# version 15
# baseline (speedup 1.0000x reference)
import numpy as np
import ml_dtypes
from contextlib import ExitStack

import concourse.bass as bass
import concourse.mybir as mybir
from concourse.bass_utils import run_bass_kernel_spmd
from concourse.alu_op_type import AluOpType as AL

F32 = mybir.dt.float32
BF16 = mybir.dt.bfloat16
AF = mybir.ActivationFunctionType
AX = mybir.AxisListType

B_LOC = 2
T = 2048
D = 1024
NBLK = 32
HP = 8
NR, MIDR = 16, 256
NS, MIDS = 2, 512
NQ, TQ = 4, 1024
EPS_LN = 1e-5
EPS_Z = 1e-6


class Ctx:

    def __init__(self, nc, es):
        self.nc = nc
        names = ["sg", "ss", "sa", "st", "sv", "sc"]
        self.h = {n: es.enter_context(nc.semaphore(n)) for n in names}
        self.c = {n: 0 for n in names}
        self.free = {}

    def inc(self, instr, n, amt=1):
        self.c[n] += amt
        instr.then_inc(self.h[n], amt)
        return (n, self.c[n])

    def now(self, n):
        return (n, self.c[n])

    def wait(self, eng, evt):
        if evt is None:
            return
        n, v = evt
        assert n in self.h and v <= self.c[n], f"bad wait {evt} (cnt={self.c.get(n)})"
        if v > 0:
            eng.wait_ge(self.h[n], v)

    def use(self, eng, res):
        self.wait(eng, self.free.get(res))

    def setfree(self, res, evt):
        self.free[res] = evt


class DB:

    def __init__(self, S, sem):
        self.S = S
        self.sem = sem
        self.items = []

    def add(self, fn, amt=1):
        self.S.c[self.sem] += amt
        evt = (self.sem, self.S.c[self.sem])
        self.items.append((fn, amt))
        return evt

    def flush(self, body_fn):
        items, S, sem = self.items, self.S, self.sem

        def run(eng):
            for fn, amt in items:
                instr = fn(eng)
                instr.then_inc(S.h[sem], amt)
        body_fn(run)
        self.items = []


def build_program(dump=None):
    nc = bass.Bass("TRN2", target_bir_lowering=False)

    x_d = nc.dram_tensor("x", [B_LOC, T, D], F32, kind="ExternalInput")
    wq_d = nc.dram_tensor("wq", [D, D], F32, kind="ExternalInput")
    wk_d = nc.dram_tensor("wk", [D, D], F32, kind="ExternalInput")
    wv_d = nc.dram_tensor("wv", [D, D], F32, kind="ExternalInput")
    wo_d = nc.dram_tensor("wo", [D, D], F32, kind="ExternalInput")
    bo_d = nc.dram_tensor("bo", [D], F32, kind="ExternalInput")
    rw_d = nc.dram_tensor("rw", [D, NR], F32, kind="ExternalInput")
    wgr_d = nc.dram_tensor("wg_r", [NR, D, MIDR], F32, kind="ExternalInput")
    wur_d = nc.dram_tensor("wu_r", [NR, D, MIDR], F32, kind="ExternalInput")
    wdr_d = nc.dram_tensor("wd_r", [NR, MIDR, D], F32, kind="ExternalInput")
    wgs_d = nc.dram_tensor("wg_s", [NS, D, MIDS], F32, kind="ExternalInput")
    wus_d = nc.dram_tensor("wu_s", [NS, D, MIDS], F32, kind="ExternalInput")
    wds_d = nc.dram_tensor("wd_s", [NS, MIDS, D], F32, kind="ExternalInput")
    cid_d = nc.dram_tensor("cident", [128, 128], BF16, kind="ExternalInput")
    cmk_d = nc.dram_tensor("cmaskt", [128, 512], BF16, kind="ExternalInput")
    out_d = nc.dram_tensor("out", [B_LOC, T, D], F32, kind="ExternalOutput")

    scr_xn = nc.dram_tensor("scr_xn", [8, 128, 4096], BF16, kind="Internal")
    scr_at = nc.dram_tensor("scr_at", [8, 128, 4096], BF16, kind="Internal")
    scr_cb = nc.dram_tensor("scr_cb", [NR, 4096], BF16, kind="Internal")
    scr_dn = nc.dram_tensor("scr_dn", [8, 128, 6144], BF16, kind="Internal")

    es = ExitStack()
    with es:
        S = Ctx(nc, es)
        block = es.enter_context(nc.Block())
        tb, vb, sb, gb, yb = block.tensor, block.vector, block.scalar, block.gpsimd, block.sync

        ident = es.enter_context(nc.sbuf_tensor("ident", [128, 128], BF16))
        maskt = es.enter_context(nc.sbuf_tensor("maskt", [128, 512], BF16))
        bo_sb = es.enter_context(nc.sbuf_tensor("bo_sb", [128, 8], F32))
        comb_fm = es.enter_context(nc.sbuf_tensor("comb_fm", [16, 4096], BF16))

        esAB = ExitStack()
        xn_fm = esAB.enter_context(nc.sbuf_tensor("xn_fm", [128, 8 * 4096], BF16))
        wq_sb = esAB.enter_context(nc.sbuf_tensor("wq_sb", [128, 8 * D], BF16))
        wk_sb = esAB.enter_context(nc.sbuf_tensor("wk_sb", [128, 8 * D], BF16))
        wv_sb = esAB.enter_context(nc.sbuf_tensor("wv_sb", [128, 8 * D], BF16))
        dstg = [esAB.enter_context(nc.sbuf_tensor(f"dstg{i}", [128, 4096], BF16))
                for i in range(2)]

        def xn_ap(k, col, ncol):
            return xn_fm.ap()[:, k * 4096 + col: k * 4096 + col + ncol]

        EV = {}

        esA = ExitStack()
        x_sb = [esA.enter_context(nc.sbuf_tensor(f"x{i}", [128, D], F32)) for i in range(2)]
        xn_tm = [esA.enter_context(nc.sbuf_tensor(f"xnt{i}", [128, D], BF16)) for i in range(2)]
        sqj = esA.enter_context(nc.sbuf_tensor("sqj", [128, D], BF16))
        sstat = esA.enter_context(nc.sbuf_tensor("sstat", [128, 2 * NBLK], F32))
        mstat = esA.enter_context(nc.sbuf_tensor("mstat", [128, NBLK], F32))
        tvar = esA.enter_context(nc.sbuf_tensor("tvar", [128, NBLK], F32))
        tm2 = esA.enter_context(nc.sbuf_tensor("tm2", [128, NBLK], F32))
        rstdb = esA.enter_context(nc.sbuf_tensor("rstdb", [128, NBLK], F32))
        nmrb = esA.enter_context(nc.sbuf_tensor("nmrb", [128, NBLK], F32))
        rw_sb = esA.enter_context(nc.sbuf_tensor("rw_sb", [128, 8 * NR], BF16))
        e4 = [esA.enter_context(nc.sbuf_tensor(f"e4{i}", [128, 4 * NR], F32))
              for i in range(2)]
        ebuf2 = esA.enter_context(nc.sbuf_tensor("ebuf2", [128, 4 * NR], F32))
        t4a = esA.enter_context(nc.sbuf_tensor("t4a", [128, 4], F32))
        t4b = esA.enter_context(nc.sbuf_tensor("t4b", [128, 4], F32))
        comb16 = esA.enter_context(nc.sbuf_tensor("comb16", [128, NBLK * NR], BF16))
        ps_tr = [esA.enter_context(nc.psum_tensor(f"pstr{i}", [128, 1024], BF16))
                 for i in range(2)]
        ps_lg = [esA.enter_context(nc.psum_tensor(f"pslg{i}", [128, 4 * NR], F32))
                 for i in range(2)]

        def _g_consts(g):
            S.inc(g.dma_start(ident[:, :], cid_d.ap()), "sg", 16)
            S.inc(g.dma_start(maskt[:, :], cmk_d.ap()), "sg", 16)
            S.inc(g.dma_start(bo_sb[:, :], bo_d.ap().rearrange("(m p) -> p m", p=128), allow_slow_non_contiguous=True),
                  "sg", 16)
            EV["consts"] = S.now("sg")
            S.inc(g.dma_start(rw_sb.ap().rearrange("p (k e) -> p k e", k=8),
                              rw_d.ap().rearrange("(k p) e -> p k e", p=128)), "sg", 16)
            EV["rwload"] = S.now("sg")
            for w_sb, w_d, key in ((wq_sb, wq_d, "wq"), (wk_sb, wk_d, "wk"),
                                   (wv_sb, wv_d, "wv")):
                S.inc(g.dma_start(w_sb.ap().rearrange("p (k m) -> p k m", k=8),
                                  w_d.ap().rearrange("(k p) m -> p k m", p=128)), "sg", 16)
                EV[key] = S.now("sg")
        gb(_g_consts)

        dbp1 = DB(S, "sc")
        for blk in range(NBLK):
            p = blk % 2

            def _sq(s, blk=blk, p=p):
                S.wait(s, EV[("a1s1", blk)])
                return s.activation(sqj[:, :], x_sb[p][:, :], AF.Square,
                                    accum_out=sstat[:, 2 * blk + 1:2 * blk + 2])
            EV[("a1sq", blk)] = dbp1.add(_sq)
            S.setfree(("x_sb", p), EV[("a1sq", blk)])
        for blk in range(NBLK):
            b, t0 = blk // 16, (blk % 16) * 128
            p = blk % 2

            def _y(sy, blk=blk, b=b, t0=t0, p=p):
                if blk >= 2:
                    S.wait(sy, EV[("a1sq", blk - 2)])
                i = sy.dma_start(x_sb[p][:, :], x_d.ap()[b, t0:t0 + 128, :])
                EV[("xld", blk)] = S.inc(i, "ss", 16)
            yb(_y)

            def _v1(v, blk=blk, p=p):
                S.wait(v, EV[("xld", blk)])
                v.memset(sstat[:, 2 * blk + 1:2 * blk + 2], 0.0)
                EV[("a1s1", blk)] = S.inc(
                    v.reduce_sum(sstat[:, 2 * blk:2 * blk + 1], x_sb[p][:, :],
                                 axis=AX.X), "sv")
            vb(_v1)
        dbp1.flush(sb)

        def _vstats(v):
            S.wait(v, EV[("a1sq", NBLK - 1)])
            s1v = bass.AP(sstat, 0, [[2 * NBLK, 128], [2, NBLK]])
            s2v = bass.AP(sstat, 1, [[2 * NBLK, 128], [2, NBLK]])
            v.tensor_scalar(mstat[:, :], s1v, 1.0 / D, None, AL.mult)
            v.tensor_scalar(tvar[:, :], s2v, 1.0 / D, None, AL.mult)
            v.drain()
            v.tensor_mul(tm2[:, :], mstat[:, :], mstat[:, :])
            v.drain()
            v.tensor_sub(tvar[:, :], tvar[:, :], tm2[:, :])
            v.drain()
            v.tensor_scalar(tvar[:, :], tvar[:, :], EPS_LN, None, AL.add)
            v.drain()
            EV["a1var"] = S.inc(v.reciprocal(tvar[:, :], tvar[:, :]), "sv")
        vb(_vstats)

        def _sstats(s):
            S.wait(s, EV["a1var"])
            EV["a1rs"] = S.inc(s.activation(rstdb[:, :], tvar[:, :], AF.Sqrt), "sc")
        sb(_sstats)

        def _vnmr(v):
            S.wait(v, EV["a1rs"])
            i = v.scalar_tensor_tensor(nmrb[:, :], mstat[:, :], -1.0, rstdb[:, :],
                                       AL.mult, AL.mult)
            EV["a1nm"] = S.inc(i, "sv")
        vb(_vnmr)

        dbp3 = DB(S, "sc")
        for blk in range(NBLK):
            p = blk % 2

            def _s3(s, blk=blk, p=p):
                S.wait(s, EV[("xld2", blk)])
                if blk == 0:
                    S.wait(s, EV["a1nm"])
                if blk >= 2:
                    S.wait(s, EV[("a1tr", blk - 2)])
                return s.activation(xn_tm[p][:, :], x_sb[p][:, :], AF.Identity,
                                    bias=nmrb[:, blk:blk + 1],
                                    scale=rstdb[:, blk:blk + 1])
            EV[("a1norm", blk)] = dbp3.add(_s3)
            S.setfree(("x_sb", p), EV[("a1norm", blk)])
        for blk in range(NBLK):
            b, t0 = blk // 16, (blk % 16) * 128
            p = blk % 2
            col = b * 2048 + t0

            def _y(sy, blk=blk, b=b, t0=t0, p=p):
                if blk >= 2:
                    S.wait(sy, EV[("a1norm", blk - 2)])
                else:
                    S.wait(sy, EV[("a1sq", NBLK - 2 + blk)])
                i = sy.dma_start(x_sb[p][:, :], x_d.ap()[b, t0:t0 + 128, :])
                EV[("xld2", blk)] = S.inc(i, "ss", 16)
            yb(_y)

            def _t1(t, blk=blk, p=p):
                if blk == 0:
                    S.wait(t, EV["consts"])
                S.wait(t, EV[("a1norm", blk)])
                S.use(t, ("ps_tr", p))
                for k in range(8):
                    i = t.transpose(ps_tr[p][:, k * 128:(k + 1) * 128],
                                    xn_tm[p][:, k * 128:(k + 1) * 128], ident[:, :])
                EV[("a1tr", blk)] = S.inc(i, "st")
                S.setfree(("xn_tm", p), EV[("a1tr", blk)])
            tb(_t1)

            def _v4(v, blk=blk, col=col, p=p):
                S.wait(v, EV[("a1tr", blk)])
                i = v.tensor_copy(
                    xn_fm.ap().rearrange("p (k c) -> p k c", k=8)[:, :, col:col + 128],
                    ps_tr[p].ap().rearrange("p (k c) -> p k c", k=8))
                EV[("a1trev", blk)] = S.inc(i, "sv")
                S.setfree(("ps_tr", p), EV[("a1trev", blk)])
            vb(_v4)

        dbp3.flush(sb)

        def _spill_xn(s):
            S.wait(s, EV[("a1trev", NBLK - 1)])
            for k in range(8):
                i = s.dma_start(scr_xn.ap()[k], xn_fm.ap()[:, k * 4096:(k + 1) * 4096])
                S.inc(i, "sa", 16)
            EV["xnspill"] = S.now("sa")
        sb(_spill_xn)

        if dump == "xn":
            def _dump(g):
                S.wait(g, EV[("a1trev", NBLK - 1)])
                dst = bass.AP(out_d, 0, [[32768, 128], [1, 32768]])
                S.inc(g.dma_start(dst, xn_fm.ap()[:, :]), "sg", 16)
                EV["dmp"] = S.now("sg")
            gb(_dump)

            def _fin(sy):
                S.wait(sy, EV["dmp"])
                S.wait(sy, EV["xnspill"])
            yb(_fin)
            esA.close()
            esAB.close()
            return nc

        dba2 = DB(S, "sc")
        for grp in range(8):
            pg = grp % 2

            def _s(s, grp=grp, pg=pg):
                S.wait(s, EV[("a2lg", grp)])
                if grp >= 2:
                    S.wait(s, EV[("a2c", grp - 2)])
                return s.activation(e4[pg][:, :], ps_lg[pg][:, :], AF.Exp)
            EV[("a2e", grp)] = dba2.add(_s)
            S.setfree(("ps_lg", pg), EV[("a2e", grp)])
        for grp in range(8):
            pg = grp % 2

            def _t(t, grp=grp, pg=pg):
                if grp == 0:
                    S.wait(t, EV[("a1trev", NBLK - 1)])
                    S.wait(t, EV["rwload"])
                if grp >= 2:
                    S.wait(t, EV[("a2e", grp - 2)])
                for j in range(4):
                    blk = grp * 4 + j
                    b, t0 = blk // 16, (blk % 16) * 128
                    col = b * 2048 + t0
                    for k in range(8):
                        i = t.matmul(ps_lg[pg][:, j * NR:(j + 1) * NR],
                                     xn_ap(k, col, 128),
                                     rw_sb[:, k * NR:(k + 1) * NR],
                                     start=(k == 0), stop=(k == 7))
                EV[("a2lg", grp)] = S.inc(i, "st")
            tb(_t)

            def _v2(v, grp=grp, pg=pg):
                S.wait(v, EV[("a2e", grp)])
                ev = e4[pg].ap().rearrange("p (j e) -> p j e", e=NR)
                e2v = ebuf2.ap().rearrange("p (j e) -> p j e", e=NR)
                v.tensor_reduce(t4a[:, :], ev, AX.X, AL.max)
                v.drain()
                m1b = t4a.ap()[:, :, None].broadcast_to([128, 4, NR])
                v.tensor_tensor(e2v, ev, m1b, AL.is_lt)
                v.tensor_mul(e2v, e2v, ev)
                v.drain()
                v.tensor_reduce(t4b[:, :], e2v, AX.X, AL.max)
                v.drain()
                m2b = t4b.ap()[:, :, None].broadcast_to([128, 4, NR])
                v.tensor_tensor(e2v, ev, m2b, AL.is_ge)
                v.tensor_add(t4a[:, :], t4a[:, :], t4b[:, :])
                v.drain()
                v.reciprocal(t4a[:, :], t4a[:, :])
                v.tensor_mul(e2v, e2v, ev)
                v.drain()
                rb = t4a.ap()[:, :, None].broadcast_to([128, 4, NR])
                i = v.tensor_tensor(
                    comb16.ap().rearrange("p (j e) -> p j e", e=NR)[:, grp * 4:grp * 4 + 4],
                    e2v, rb, AL.mult)
                EV[("a2c", grp)] = S.inc(i, "sv")
                S.setfree(("e4", pg), EV[("a2c", grp)])
            vb(_v2)

        dba2.flush(sb)

        for grp in range(4):
            def _t(t, grp=grp):
                S.wait(t, EV[("a2c", grp * 2 + 1)])
                S.use(t, ("ps_tr", grp % 2))
                for j in range(8):
                    blk = grp * 8 + j
                    i = t.transpose(ps_tr[grp % 2][0:16, j * 128:(j + 1) * 128],
                                    comb16[:, blk * NR:(blk + 1) * NR], ident[:, :])
                EV[("a2ctr", grp)] = S.inc(i, "st")
            tb(_t)

            def _v(v, grp=grp):
                S.wait(v, EV[("a2ctr", grp)])
                i = v.tensor_copy(comb_fm.ap()[:, grp * 1024:(grp + 1) * 1024],
                                  ps_tr[grp % 2].ap()[0:16, :])
                EV[("a2ctrev", grp)] = S.inc(i, "sv")
                S.setfree(("ps_tr", grp % 2), EV[("a2ctrev", grp)])
            vb(_v)


        def _spill_cb(s):
            S.wait(s, EV[("a2ctrev", 3)])
            i = s.dma_start(scr_cb.ap()[:, :], comb_fm.ap()[:, :])
            EV["cbspill"] = S.inc(i, "sa", 16)
        sb(_spill_cb)

        if dump == "comb":
            def _dump(g):
                S.wait(g, EV[("a2c", 7)])
                dst = bass.AP(out_d, 0, [[NBLK * NR, 128], [1, NBLK * NR]])
                S.inc(g.dma_start(dst, comb16.ap()[:, :]), "sg", 16)
                EV["dmp"] = S.now("sg")
            gb(_dump)

            def _fin(sy):
                S.wait(sy, EV["dmp"])
                S.wait(sy, EV["cbspill"])
            yb(_fin)
            esA.close()
            esAB.close()
            return nc

        items = [("wdr", e) for e in range(NR)] + [("wds", e2) for e2 in range(NS)] \
            + [("wo", 0), ("wo", 1)]
        for idx, (kind, e) in enumerate(items):
            p = idx % 2

            def _g(g, kind=kind, e=e, p=p, idx=idx):
                S.use(g, ("dstg", p))
                if kind == "wdr":
                    src = wdr_d.ap()[e].rearrange("(k p) c -> p k c", p=128)
                    dst = dstg[p].ap()[:, 0:2048].rearrange("p (k c) -> p k c", k=2)
                elif kind == "wds":
                    src = wds_d.ap()[e].rearrange("(k p) c -> p k c", p=128)
                    dst = dstg[p].ap().rearrange("p (k c) -> p k c", k=4)
                else:
                    src = wo_d.ap().rearrange("(k p) c -> p k c", p=128)[:, e * 4:(e + 1) * 4, :]
                    dst = dstg[p].ap().rearrange("p (k c) -> p k c", k=4)
                EV[("ppr", idx)] = S.inc(g.dma_start(dst, src), "sg", 16)
            gb(_g)

            def _s(s, kind=kind, e=e, p=p, idx=idx):
                S.wait(s, EV[("ppr", idx)])
                nk = 2 if kind == "wdr" else 4
                for dm in range(8):
                    off = {"wdr": e * 256, "wds": 4096 + e * 512, "wo": 5120 + e * 512}[kind]
                    src = bass.AP(dstg[p], dm * 128, [[4096, 128], [1024, nk], [1, 128]])
                    dst = bass.AP(scr_dn, dm * 128 * 6144 + off,
                                  [[6144, 128], [128, nk], [1, 128]])
                    i = s.dma_start(dst, src)
                    S.inc(i, "sa", 16)
                EV[("ppw", idx)] = S.now("sa")
                S.setfree(("dstg", p), EV[("ppw", idx)])
            sb(_s)
        EV["ppdone"] = EV[("ppw", len(items) - 1)]

        esA.close()
        esB = ExitStack()
        qraw = esB.enter_context(nc.sbuf_tensor("qraw", [128, T], BF16))
        kraw = esB.enter_context(nc.sbuf_tensor("kraw", [128, T], BF16))
        etmp = [esB.enter_context(nc.sbuf_tensor(f"etmp{i}", [128, T], BF16))
                for i in range(2)]
        mtmp = [esB.enter_context(nc.sbuf_tensor(f"mtmp{i}", [128, T], BF16))
                for i in range(2)]
        qf = esB.enter_context(nc.sbuf_tensor("qf", [128, T], BF16))
        kf = esB.enter_context(nc.sbuf_tensor("kf", [128, T], BF16))
        vf = esB.enter_context(nc.sbuf_tensor("vf", [128, T], BF16))
        q_tm = esB.enter_context(nc.sbuf_tensor("q_tm", [128, T], BF16))
        k_tm = esB.enter_context(nc.sbuf_tensor("k_tm", [128, T], BF16))
        v_tm = esB.enter_context(nc.sbuf_tensor("v_tm", [128, T], BF16))
        apack = esB.enter_context(nc.sbuf_tensor("apack", [128, T], BF16))
        aofm = esB.enter_context(nc.sbuf_tensor("aofm", [128, T], BF16))
        cumst = esB.enter_context(nc.sbuf_tensor("cumst", [128, 1024], F32))
        zt = esB.enter_context(nc.sbuf_tensor("zt", [128, 1024], F32))
        zq = esB.enter_context(nc.sbuf_tensor("zq", [128, 32], F32))
        sc_sb = [esB.enter_context(nc.sbuf_tensor(f"scsb{i}", [128, 1024], BF16))
                 for i in range(2)]
        st_b = [esB.enter_context(nc.sbuf_tensor(f"st_b{i}", [128, 128], BF16))
                for i in range(2)]
        ps_p = [esB.enter_context(nc.psum_tensor(f"psp{i}", [128, 512], F32))
                for i in range(2)]
        ps_sc = [esB.enter_context(nc.psum_tensor(f"pssc{i}", [128, 512], F32))
                 for i in range(2)]
        ps_un = esB.enter_context(nc.psum_tensor("psun", [128, 512], F32))
        ps_st = [esB.enter_context(nc.psum_tensor(f"psst{i}", [128, 64], F32))
                 for i in range(2)]
        ps_trB = esB.enter_context(nc.psum_tensor("pstrB", [128, 1024], BF16))

        FENCE_A = [EV[("a2ctrev", 3)], EV[("a2ctr", 3)], EV[("a2e", 7)]]

        fidx = 0
        ridx = 0

        for b in range(B_LOC):
            def _vz(v, b=b):
                S.use(v, "cumst")
                if b == 0:
                    for e_ in FENCE_A:
                        S.wait(v, e_)
                    S.inc(v.memset(st_b[0][:, :], 0.0), "sv")
                    S.inc(v.memset(st_b[1][:, :], 0.0), "sv")
                S.inc(v.memset(cumst[:, :], 0.0), "sv")
            vb(_vz)

            for hp in range(HP):
                hpi = b * HP + hp
                dbhp = DB(S, "sc")
                qbase = fidx
                ubase = 2 * qbase
                for j6, (dst, dname) in enumerate(((qraw, "qraw"), (qraw, "qraw"),
                                                   (kraw, "kraw"), (kraw, "kraw"),
                                                   (vf, "vf"), (vf, "vf"))):
                    half = j6 % 2
                    for n in range(2):
                        u = ubase + 2 * j6 + n
                        slot = u % 2

                        frev = S.free.get(dname) if (half == 0 and n == 0) else None

                        def _s(s, dst=dst, half=half, n=n, slot=slot, u=u, frev=frev):
                            S.wait(s, EV[("pfill", u)])
                            if u == 0:
                                for e_ in FENCE_A:
                                    S.wait(s, e_)
                            S.wait(s, frev)
                            return s.copy(
                                dst[:, half * 1024 + n * 512: half * 1024 + (n + 1) * 512],
                                ps_p[slot][:, :])
                        EV[("pevac", u)] = dbhp.add(_s)
                        S.setfree(("ps_p", slot), EV[("pevac", u)])
                for fkey, mslot in (("qfeat", 0), ("kfeat", 1)):
                    for hh in range(2):
                        def _s(s, fkey=fkey, mslot=mslot, hh=hh, hpi=hpi):
                            S.wait(s, EV[("ftmin", (hpi, mslot, hh))])
                            cl = hh * 1024
                            return s.activation(etmp[mslot][:, cl:cl + 1024],
                                                mtmp[mslot][:, cl:cl + 1024], AF.Exp)
                        EV[("ftexp", (hpi, mslot, hh))] = dbhp.add(_s)

                for (wsb, wkey) in ((wq_sb, "wq"), (wk_sb, "wk"), (wv_sb, "wv")):
                    for half in range(2):
                        myf = fidx

                        def _t(t, wsb=wsb, wkey=wkey, half=half, myf=myf,
                               b=b, hp=hp):
                            u0 = 2 * myf
                            if myf == 0:
                                for e_ in FENCE_A:
                                    S.wait(t, e_)
                            if myf < 6:
                                S.wait(t, EV[wkey])
                                S.wait(t, EV[("a1trev", NBLK - 1)])
                            col = b * 2048 + half * 1024
                            for n in range(2):
                                u = u0 + n
                                if u >= 2:
                                    S.wait(t, EV[("pevac", u - 2)])
                                for k in range(8):
                                    i = t.matmul(ps_p[u % 2][:, :],
                                                 wsb[:, k * D + hp * 128: k * D + hp * 128 + 128],
                                                 xn_ap(k, col + n * 512, 512),
                                                 start=(k == 0), stop=(k == 7))
                                EV[("pfill", u)] = S.inc(i, "st")
                        tb(_t)
                        fidx += 1

                for (raw, feat, ubase_r, fkey, mslot) in (
                        (qraw, qf, ubase, "qfeat", 0),
                        (kraw, kf, ubase + 4, "kfeat", 1)):
                    for hh in range(2):
                        def _v(v, raw=raw, ubase_r=ubase_r, hh=hh, mslot=mslot,
                               hpi=hpi):
                            S.wait(v, EV[("pevac", ubase_r + 2 * hh + 1)])
                            if hpi == 0 and hh == 0:
                                for e_ in FENCE_A:
                                    S.wait(v, e_)
                            if hh == 0:
                                S.use(v, ("mtmp", mslot))
                            cl = hh * 1024
                            EV[("ftmin", (hpi, mslot, hh))] = S.inc(
                                v.tensor_scalar(mtmp[mslot][:, cl:cl + 1024],
                                                raw[:, cl:cl + 1024], 0.0, None,
                                                AL.min), "sv")
                        vb(_v)

                        def _v2(v, raw=raw, feat=feat, fkey=fkey, hh=hh,
                                mslot=mslot, hpi=hpi):
                            S.wait(v, EV[("ftexp", (hpi, mslot, hh))])
                            if hh == 0:
                                S.use(v, fkey)
                            cl = hh * 1024
                            i = v.scalar_tensor_tensor(feat[:, cl:cl + 1024],
                                                       raw[:, cl:cl + 1024], 0.0,
                                                       etmp[mslot][:, cl:cl + 1024],
                                                       AL.max, AL.add)
                            EV[(fkey, (hpi, hh))] = S.inc(i, "sv")
                            if hh == 1:
                                S.setfree(("mtmp", mslot), EV[(fkey, (hpi, hh))])
                        vb(_v2)
                dbhp.flush(sb)

                def _rawfree(hpi=hpi):
                    S.setfree("qraw", EV[("qfeat", (hpi, 1))])
                    S.setfree("kraw", EV[("kfeat", (hpi, 1))])
                _rawfree()

                for (src, dst, need, dkey) in ((qf, q_tm, "qfeat", "q_tm"),
                                               (kf, k_tm, "kfeat", "k_tm"),
                                               (vf, v_tm, "pevac", "v_tm")):
                    for r in range(2):
                        myr = ridx

                        def _t(t, src=src, r=r, myr=myr, need=need, dkey=dkey,
                               hpi=hpi):
                            if need == "pevac":
                                S.wait(t, EV[("pevac", ubase + 9 + 2 * r)])
                            else:
                                S.wait(t, EV[(need, (hpi, r))])
                            S.use(t, "ps_trB")
                            for j in range(8):
                                c = r * 8 + j
                                i = t.transpose(ps_trB[:, j * 128:(j + 1) * 128],
                                                src[:, c * 128:(c + 1) * 128], ident[:, :])
                            EV[("trB", myr)] = S.inc(i, "st")
                            if dkey == "v_tm" and r == 1:
                                S.setfree("vf", EV[("trB", myr)])
                        tb(_t)

                        def _v(v, dst=dst, r=r, myr=myr, dkey=dkey):
                            S.wait(v, EV[("trB", myr)])
                            if r == 0:
                                S.use(v, dkey)
                            i = v.tensor_copy(dst[:, r * 1024:(r + 1) * 1024], ps_trB[:, :])
                            EV[("trBev", myr)] = S.inc(i, "sv")
                            S.setfree("ps_trB", EV[("trBev", myr)])
                        vb(_v)
                        ridx += 1

                def _freefeat(hpi=hpi):
                    pass
                _freefeat()

                for h2 in range(2):
                    h_off = h2 * 64

                    def _vz(v, b=b, hp=hp, h2=h2, h_off=h_off, hpi=hpi):
                        S.wait(v, EV[("trBev", ridx - 1)])
                        km = bass.AP(k_tm, h_off, [[T, 128], [128, 16], [1, 64]])
                        cs = cumst.ap().rearrange("p (c d) -> p c d", d=64)
                        v.tensor_tensor(cs, cs, km, AL.add)
                        qm = bass.AP(q_tm, h_off, [[T, 128], [128, 16], [1, 64]])
                        ztv = zt.ap().rearrange("p (c d) -> p c d", d=64)
                        v.tensor_tensor(ztv, qm, cs, AL.mult)
                        v.tensor_reduce(zq[:, h2 * 16:h2 * 16 + 16], ztv, AX.X, AL.add)
                        v.drain()
                        v.tensor_scalar(zq[:, h2 * 16:h2 * 16 + 16],
                                        zq[:, h2 * 16:h2 * 16 + 16], EPS_Z, None, AL.add)
                        v.drain()
                        i = v.reciprocal(zq[:, h2 * 16:h2 * 16 + 16],
                                         zq[:, h2 * 16:h2 * 16 + 16])
                        EV[("zr", (hpi, h2))] = S.inc(i, "sv")
                        if hp == HP - 1 and h2 == 1:
                            S.setfree("cumst", EV[("zr", (hpi, h2))])
                    vb(_vz)


                def emit_scores(w, hpi=hpi):
                    for h2 in range(2):
                        h_off = h2 * 64

                        def _tsc(t, h2=h2, h_off=h_off, w=w, hpi=hpi):
                            if hpi == 0 and w == 0 and h2 == 0:
                                S.wait(t, EV["consts"])
                            S.use(t, ("ps_sc", h2))
                            for j in range(4):
                                c = 4 * w + j
                                i = t.matmul(ps_sc[h2][:, j * 128:(j + 1) * 128],
                                             kf[h_off:h_off + 64, c * 128:(c + 1) * 128],
                                             qf[h_off:h_off + 64, c * 128:(c + 1) * 128],
                                             start=True, stop=True)
                            EV[("sc", (hpi, w, h2))] = S.inc(i, "st")
                        tb(_tsc)

                        def _vsc(v, w=w, h2=h2, hpi=hpi):
                            S.wait(v, EV[("sc", (hpi, w, h2))])
                            S.use(v, ("sc_sb", (w % 2, h2)))
                            i = v.tensor_tensor(
                                sc_sb[w % 2][:, h2 * 512:(h2 + 1) * 512],
                                ps_sc[h2][:, :], maskt[:, :], AL.mult)
                            EV[("scev", (hpi, w, h2))] = S.inc(i, "sv")
                            S.setfree(("ps_sc", h2), EV[("scev", (hpi, w, h2))])
                        vb(_vsc)

                def emit_jloop(w, hpi=hpi):
                    for j in range(4):
                        c = 4 * w + j
                        for h2 in range(2):
                            h_off = h2 * 64

                            def _tc(t, h2=h2, h_off=h_off, w=w, j=j, c=c, hpi=hpi):
                                S.wait(t, EV[("scev", (hpi, w, h2))])
                                if j == 0:
                                    S.use(t, ("ps_un", h2))
                                reg = ps_un[:, j * 128 + h2 * 64: j * 128 + h2 * 64 + 64]
                                i = t.matmul(reg,
                                             sc_sb[w % 2][:, h2 * 512 + j * 128:
                                                          h2 * 512 + (j + 1) * 128],
                                             v_tm[:, c * 128 + h_off: c * 128 + h_off + 64],
                                             start=(h2 == 0),
                                             stop=(c == 0 and h2 == 1))
                                EV[("av", (hpi, h2, c))] = S.inc(i, "st")
                                if c < 15:
                                    if c == 0:
                                        S.use(t, ("ps_st", h2))
                                    else:
                                        S.wait(t, EV[("stc", (hpi, h2, c - 1))])
                                    tp = (0, 64) if h2 == 1 else (0, 0)
                                    i = t.matmul(ps_st[h2][h_off:h_off + 64, :],
                                                 k_tm[:, c * 128 + h_off: c * 128 + h_off + 64],
                                                 v_tm[:, c * 128 + h_off: c * 128 + h_off + 64],
                                                 start=(c == 0), stop=(c == 14),
                                                 tile_position=tp)
                                    EV[("dl", (hpi, h2, c))] = S.inc(i, "st")
                            tb(_tc)

                        if c > 0:
                            def _tt2(t, w=w, j=j, c=c, hpi=hpi):
                                S.wait(t, EV[("stc", (hpi, 0, c - 1))])
                                S.wait(t, EV[("stc", (hpi, 1, c - 1))])
                                i = t.matmul(ps_un[:, j * 128:(j + 1) * 128],
                                             qf[:, c * 128:(c + 1) * 128],
                                             st_b[(c - 1) % 2][:, :],
                                             start=False, stop=True)
                                EV[("t2", (hpi, c))] = S.inc(i, "st")
                                S.setfree(("stb", (c - 1) % 2),
                                          EV[("t2", (hpi, c))])
                            tb(_tt2)

                        if c < 15:
                            for h2 in range(2):
                                h_off = h2 * 64

                                def _sst(v, h_off=h_off, c=c, hpi=hpi, h2=h2):
                                    S.wait(v, EV[("dl", (hpi, h2, c))])
                                    S.use(v, ("stb", c % 2))
                                    i = v.tensor_copy(st_b[c % 2][h_off:h_off + 64,
                                                                  h_off:h_off + 64],
                                                      ps_st[h2][h_off:h_off + 64, :])
                                    EV[("stc", (hpi, h2, c))] = S.inc(i, "sv")
                                    if c == 14:
                                        S.setfree(("ps_st", h2),
                                                  EV[("stc", (hpi, h2, c))])
                                vb(_sst)

                    for h2 in range(2):
                        h_off = h2 * 64

                        def _vzev(v, h2=h2, h_off=h_off, w=w, hpi=hpi):
                            S.wait(v, EV[("t2", (hpi, 4 * w + 3))])
                            S.wait(v, EV[("zr", (hpi, h2))])
                            if w == 0 and h2 == 0:
                                S.use(v, "apack")
                            dst = bass.AP(apack, w * 512 + h_off,
                                          [[T, 128], [128, 4], [1, 64]])
                            src = bass.AP(ps_un, h2 * 64, [[512, 128], [128, 4], [1, 64]])
                            zb = zq.ap()[:, h2 * 16 + 4 * w: h2 * 16 + 4 * w + 4]
                            i = v.tensor_tensor(dst, src,
                                                zb[:, :, None].broadcast_to([128, 4, 64]),
                                                AL.mult)
                            EV[("zev", (hpi, w, h2))] = S.inc(i, "sv")
                            S.setfree(("ps_un", h2), EV[("zev", (hpi, w, h2))])
                            S.setfree(("sc_sb", (w % 2, h2)), EV[("zev", (hpi, w, h2))])
                        vb(_vzev)

                for w in range(4):
                    emit_scores(w)
                    emit_jloop(w)

                last_t2 = EV[("t2", (hpi, 15))]
                S.setfree("qfeat", last_t2)
                S.setfree("kfeat", last_t2)
                S.setfree("q_tm", last_t2)
                S.setfree("k_tm", last_t2)
                S.setfree("v_tm", last_t2)

                for r in range(2):
                    myr = ridx

                    def _t(t, r=r, myr=myr, hpi=hpi):
                        S.wait(t, EV[("zev", (hpi, 3, 1))])
                        S.use(t, "ps_trB")
                        for j in range(8):
                            c = r * 8 + j
                            i = t.transpose(ps_trB[:, j * 128:(j + 1) * 128],
                                            apack[:, c * 128:(c + 1) * 128], ident[:, :])
                        EV[("trB", myr)] = S.inc(i, "st")
                        if r == 1:
                            S.setfree("apack", EV[("trB", myr)])
                    tb(_t)

                    def _v(v, r=r, myr=myr):
                        S.wait(v, EV[("trB", myr)])
                        if r == 0:
                            S.use(v, "aofm")
                        i = v.tensor_copy(aofm[:, r * 1024:(r + 1) * 1024], ps_trB[:, :])
                        EV[("trBev", myr)] = S.inc(i, "sv")
                        S.setfree("ps_trB", EV[("trBev", myr)])
                    vb(_v)
                    ridx += 1

                def _sp(s, b=b, hp=hp, hpi=hpi):
                    S.wait(s, EV[("trBev", ridx - 1)])
                    i = s.dma_start(scr_at.ap()[hp][:, b * 2048:(b + 1) * 2048], aofm[:, :])
                    EV[("aspill", hpi)] = S.inc(i, "sa", 16)
                    S.setfree("aofm", EV[("aspill", hpi)])
                sb(_sp)

                if dump == "attn":
                    def _dmp(g, b=b, hp=hp, hpi=hpi):
                        S.wait(g, EV[("trBev", ridx - 1)])
                        S.wait(g, EV[("aspill", hpi)])
                        off = (hp * 2 + b) * 128 * 2048
                        dst = bass.AP(out_d, off, [[2048, 128], [1, 2048]])
                        i = g.dma_start(dst, aofm[:, :])
                        EV[("adump", hpi)] = S.inc(i, "sg", 16)
                        S.setfree("aofm", EV[("adump", hpi)])
                    gb(_dmp)

        EV["bdone_v"] = S.now("sv")
        EV["bdone_t"] = S.now("st")
        EV["bdone_s"] = S.now("sc")
        EV["bdone_a"] = S.now("sa")
        EV["bdone_g"] = S.now("sg")
        EV["bdone_y"] = S.now("ss")

        if dump == "attn":
            def _fin(sy):
                S.wait(sy, EV["bdone_a"])
                S.wait(sy, EV["bdone_g"])
                S.wait(sy, EV["bdone_y"])
            yb(_fin)
            esB.close()
            esAB.close()
            return nc

        esB.close()
        esAB.close()
        esC = ExitStack()
        h_sb = [esC.enter_context(nc.sbuf_tensor(f"h{e}", [128, 2048], BF16))
                for e in range(NR)]
        hs_sb = esC.enter_context(nc.sbuf_tensor("hs", [128, 8 * 1024], BF16))
        xn_q = esC.enter_context(nc.sbuf_tensor("xnq", [128, 8 * 1024], BF16))
        at_q = esC.enter_context(nc.sbuf_tensor("atq", [128, 8 * 1024], BF16))
        wgu = [esC.enter_context(nc.sbuf_tensor(f"wgu{i}", [128, 4096], BF16))
               for i in range(2)]
        dstr = [esC.enter_context(nc.sbuf_tensor(f"dstr{i}", [128, 6144], BF16))
                for i in range(2)]
        crep = [esC.enter_context(nc.sbuf_tensor(f"crep{i}", [128, 1024], BF16))
                for i in range(2)]
        sgb = [esC.enter_context(nc.sbuf_tensor(f"sgb{i}", [128, 1024], BF16))
               for i in range(2)]
        htmp = esC.enter_context(nc.sbuf_tensor("htmp", [128, 1024], BF16))
        outbf = esC.enter_context(nc.sbuf_tensor("outbf", [128, 8 * 1024], BF16))
        xt_sb = [esC.enter_context(nc.sbuf_tensor(f"xt{i}", [128, D], F32))
                 for i in range(2)]
        of_sb = [esC.enter_context(nc.sbuf_tensor(f"of{i}", [128, D], F32))
                 for i in range(2)]
        ps_up = [esC.enter_context(nc.psum_tensor(f"psup{i}", [128, 1024], F32))
                 for i in range(2)]
        ps_dn = esC.enter_context(nc.psum_tensor("psdn", [128, 1024], F32))
        ps_trC = [esC.enter_context(nc.psum_tensor(f"pstrC{i}", [128, 1024], BF16))
                  for i in range(2)]

        FENCE_B = [EV["bdone_v"], EV["bdone_t"], EV["bdone_s"], EV["bdone_a"],
                   EV["bdone_g"], EV["bdone_y"]]

        def fence(eng):
            for e_ in FENCE_B:
                S.wait(eng, e_)

        eidx = 0

        def up_fill(t, slot, wslot, coff, q, extra_waits):
            for e_ in extra_waits:
                S.wait(t, e_)
            S.use(t, ("ps_up", slot))
            for k in range(8):
                for n in range(2):
                    i = t.matmul(ps_up[slot][:, n * 512:(n + 1) * 512],
                                 wgu[wslot][:, k * 512 + coff: k * 512 + coff + 128],
                                 xn_q[:, k * 1024 + n * 512: k * 1024 + (n + 1) * 512],
                                 start=(k == 0), stop=(k == 7))
            return S.inc(i, "st")

        pending_final = [None]
        pending_xt = [None]
        for q in range(NQ):
            b = q // 2
            tq0 = (q % 2) * 1024

            if pending_xt[0] is not None:
                pending_xt[0]()
                pending_xt[0] = None

            def _yq(sy, q=q):
                if q == 0:
                    fence(sy)
                S.wait(sy, EV["xnspill"])
                S.use(sy, "xn_q")
                src = bass.AP(scr_xn, q * 1024, [[4096, 128], [128 * 4096, 8], [1, 1024]])
                i = sy.dma_start(xn_q.ap().rearrange("p (k c) -> p k c", k=8), src)
                EV[("xnq", q)] = S.inc(i, "ss", 16)
                S.use(sy, "at_q")
                src = bass.AP(scr_at, q * 1024, [[4096, 128], [128 * 4096, 8], [1, 1024]])
                i = sy.dma_start(at_q.ap().rearrange("p (k c) -> p k c", k=8), src)
                EV[("atq", q)] = S.inc(i, "ss", 16)
            yb(_yq)

            for e2 in range(NS):
                for proj, wd in ((0, wgs_d), (1, wus_d)):
                    def _g(g, wd=wd, e2=e2, proj=proj, q=q):
                        if q == 0 and e2 == 0 and proj == 0:
                            fence(g)
                        S.use(g, ("wgu", proj))
                        src = wd.ap()[e2].rearrange("(k p) m -> p k m", p=128)
                        i = g.dma_start(wgu[proj].ap().rearrange("p (k m) -> p k m", k=8),
                                        src)
                        EV[("shw", (q, e2, proj))] = S.inc(i, "sg", 16)
                    gb(_g)

                dbsh = DB(S, "sc")
                for m in range(4):
                    frev = S.free.get(("sgb", 0)) if m == 0 else None

                    def _s(s, e2=e2, m=m, q=q, frev=frev):
                        S.wait(s, EV[("upf", (q, "sh", e2, m, 0))])
                        if q == 0 and e2 == 0 and m == 0:
                            fence(s)
                        if m == 0:
                            S.wait(s, frev)
                        else:
                            S.wait(s, EV[("hsev", (q, e2, m - 1))])
                        return s.activation(sgb[0][:, :], ps_up[0][:, :], AF.Silu)
                    EV[("shsilu", (q, e2, m))] = dbsh.add(_s)
                for m in range(4):
                    for proj in range(2):
                        def _t(t, e2=e2, m=m, proj=proj, q=q):
                            if q == 0 and e2 == 0 and m == 0 and proj == 0:
                                fence(t)
                            ew = [EV[("shw", (q, e2, proj))], EV[("xnq", q)]]
                            EV[("upf", (q, "sh", e2, m, proj))] = up_fill(
                                t, proj, proj, m * 128, q, ew)
                            if m == 3:
                                S.setfree(("wgu", proj),
                                          EV[("upf", (q, "sh", e2, m, proj))])
                        tb(_t)
                        if proj == 0:
                            S.setfree(("ps_up", 0), EV[("shsilu", (q, e2, m))])
                        else:
                            pass

                    def _v(v, e2=e2, m=m, q=q):
                        S.wait(v, EV[("upf", (q, "sh", e2, m, 1))])
                        S.wait(v, EV[("shsilu", (q, e2, m))])
                        if q == 0 and e2 == 0 and m == 0:
                            fence(v)
                        if e2 == 0 and m == 0:
                            S.use(v, "hs_sb")
                        col = (e2 * 4 + m) * 1024
                        i = v.tensor_tensor(hs_sb[:, col:col + 1024], ps_up[1][:, :],
                                            sgb[0][:, :], AL.mult)
                        EV[("hsev", (q, e2, m))] = S.inc(i, "sv")
                        S.setfree(("ps_up", 1), EV[("hsev", (q, e2, m))])
                    vb(_v)
                dbsh.flush(sb)

            if pending_final[0] is not None:
                pending_final[0]()
                pending_final[0] = None

            for e in range(NR):
                mye = eidx
                ep = mye % 2

                def _g(g, e=e, mye=mye, ep=ep, q=q):
                    S.use(g, ("wgu", ep))
                    src = wgr_d.ap()[e].rearrange("(k p) m -> p k m", p=128)
                    dst = bass.AP(wgu[ep], 0, [[4096, 128], [512, 8], [1, 256]])
                    S.inc(g.dma_start(dst, src), "sg", 16)
                    src = wur_d.ap()[e].rearrange("(k p) m -> p k m", p=128)
                    dst = bass.AP(wgu[ep], 256, [[4096, 128], [512, 8], [1, 256]])
                    EV[("rw", mye)] = S.inc(g.dma_start(dst, src), "sg", 16)
                    S.wait(g, EV["cbspill"])
                    S.use(g, ("crep", ep))
                    src = bass.AP(scr_cb, e * 4096 + q * 1024, [[0, 128], [1, 1024]])
                    EV[("crep", mye)] = S.inc(g.dma_start(crep[ep][:, :], src), "sg", 16)
                gb(_g)

                dbrt = DB(S, "sc")
                for m2 in range(2):
                    frev = S.free.get(("sgb", m2))

                    def _s(s, mye=mye, m=m2, slot=m2, frev=frev):
                        S.wait(s, EV[("rupf", (mye, m))])
                        S.wait(s, frev)
                        return s.activation(sgb[m][:, :], ps_up[slot][:, :], AF.Silu)
                    EV[("silu", (mye, m2))] = dbrt.add(_s)
                for rnd in range(4):
                    proj = 0 if rnd < 2 else 1
                    m = rnd % 2
                    slot = rnd % 2
                    coff = proj * 256 + m * 128

                    def _t(t, e=e, mye=mye, ep=ep, rnd=rnd, m=m, slot=slot, coff=coff, q=q):
                        ew = [EV[("rw", mye)], EV[("xnq", q)]]
                        evt = up_fill(t, slot, ep, coff, q, ew)
                        EV[("rupf", (mye, rnd))] = evt
                        if rnd == 3:
                            S.setfree(("wgu", ep), evt)
                            if e == NR - 1:
                                EV[("upte", q)] = evt
                                S.setfree("xn_q", evt)
                    tb(_t)
                    if rnd < 2:
                        S.setfree(("ps_up", rnd), EV[("silu", (mye, rnd))])

                    if rnd < 2:
                        pass
                    else:
                        def _v(v, e=e, mye=mye, ep=ep, m=m, rnd=rnd, slot=slot, q=q):
                            S.wait(v, EV[("rupf", (mye, rnd))])
                            S.wait(v, EV[("crep", mye)])
                            if rnd == 2:
                                S.use(v, ("h_sb", e))
                            i1 = v.tensor_tensor(htmp[:, :], ps_up[slot][:, :],
                                                 crep[ep][:, :], AL.mult)
                            S.setfree(("ps_up", slot), S.inc(i1, "sv"))
                            S.wait(v, EV[("silu", (mye, m))])
                            i = v.tensor_tensor(h_sb[e][:, m * 1024:(m + 1) * 1024],
                                                htmp[:, :], sgb[m][:, :], AL.mult)
                            EV[("hmul", (mye, m))] = S.inc(i, "sv")
                            S.setfree(("sgb", m), EV[("hmul", (mye, m))])
                            if m == 1:
                                S.setfree(("crep", ep), EV[("hmul", (mye, m))])
                                if e == NR - 1:
                                    EV[("hall", q)] = EV[("hmul", (mye, m))]
                        vb(_v)
                dbrt.flush(sb)
                eidx += 1

            dbdn = DB(S, "sc")
            for dm in range(8):
                mydm = q * 8 + dm

                frev = S.free.get("outbf") if dm == 0 else None

                def _sdn(s, dm=dm, mydm=mydm, q=q, frev=frev):
                    S.wait(s, EV[("dnmm", mydm)])
                    S.wait(s, frev)
                    return s.activation(outbf[:, dm * 1024:(dm + 1) * 1024],
                                        ps_dn[:, :], AF.Identity,
                                        bias=bo_sb[:, dm:dm + 1])
                EV[("dnev", mydm)] = dbdn.add(_sdn)
            for dm in range(8):
                mydm = q * 8 + dm
                dp = mydm % 2

                def _y(sy, dm=dm, mydm=mydm, dp=dp):
                    S.wait(sy, EV["ppdone"])
                    S.use(sy, ("dstr", dp))
                    i = sy.dma_start(dstr[dp][:, :], scr_dn.ap()[dm])
                    EV[("dst", mydm)] = S.inc(i, "ss", 16)
                yb(_y)

                def _t(t, dm=dm, mydm=mydm, dp=dp, q=q):
                    S.wait(t, EV[("dst", mydm)])
                    S.wait(t, EV[("hall", q)])
                    S.wait(t, EV[("hsev", (q, NS - 1, 3))])
                    S.wait(t, EV[("atq", q)])
                    S.use(t, "ps_dn")
                    srcs = []
                    for e in range(NR):
                        for k2 in range(2):
                            srcs.append(((e * 2 + k2) * 128,
                                         h_sb[e].ap()[:, k2 * 1024:(k2 + 1) * 1024]))
                    for e2 in range(NS):
                        for k2 in range(4):
                            srcs.append((4096 + (e2 * 4 + k2) * 128,
                                         hs_sb.ap()[:, (e2 * 4 + k2) * 1024:
                                                    (e2 * 4 + k2 + 1) * 1024]))
                    for k in range(8):
                        srcs.append((5120 + k * 128, at_q.ap()[:, k * 1024:(k + 1) * 1024]))
                    for si, (off, rhs) in enumerate(srcs):
                        for n in range(2):
                            i = t.matmul(ps_dn[:, n * 512:(n + 1) * 512],
                                         dstr[dp][:, off:off + 128],
                                         rhs[:, n * 512:(n + 1) * 512],
                                         start=(si == 0), stop=(si == len(srcs) - 1))
                    EV[("dnmm", mydm)] = S.inc(i, "st")
                    S.setfree("ps_dn", EV[("dnev", mydm)])
                    S.setfree(("dstr", dp), EV[("dnmm", mydm)])
                    if dm == 7:
                        EV[("dnte", q)] = EV[("dnmm", mydm)]
                        for e in range(NR):
                            S.setfree(("h_sb", e), EV[("dnte", q)])
                        S.setfree("hs_sb", EV[("dnte", q)])
                        S.setfree("at_q", EV[("dnte", q)])
                tb(_t)


            dbdn.flush(sb)

            def _emit_xt(q=q, b=b, tq0=tq0, ls=(0, 1)):
                for l in ls:
                    li = q * 8 + l
                    lp = li % 2
                    t0 = tq0 + l * 128

                    def _y(sy, q=q, b=b, l=l, li=li, lp=lp, t0=t0):
                        S.use(sy, ("xt", lp))
                        i = sy.dma_start(xt_sb[lp][:, :], x_d.ap()[b, t0:t0 + 128, :])
                        EV[("xt", li)] = S.inc(i, "ss", 16)
                    yb(_y)

            def _emit_final(q=q, b=b, tq0=tq0):
                for l in range(8):
                    li = q * 8 + l
                    lp = li % 2
                    t0 = tq0 + l * 128
                    if l >= 2:
                        _emit_xt(q, b, tq0, ls=(l,))

                    def _t(t, q=q, l=l, li=li, lp=lp):
                        S.wait(t, EV[("dnev", q * 8 + 7)])
                        S.use(t, ("ps_trC", lp))
                        for dm in range(8):
                            i = t.transpose(ps_trC[lp][:, dm * 128:(dm + 1) * 128],
                                            outbf[:, dm * 1024 + l * 128: dm * 1024 + l * 128 + 128],
                                            ident[:, :])
                        EV[("ftr", li)] = S.inc(i, "st")
                        if l == 7:
                            S.setfree("outbf", EV[("ftr", li)])
                    tb(_t)

                    def _v(v, li=li, lp=lp):
                        S.wait(v, EV[("ftr", li)])
                        S.wait(v, EV[("xt", li)])
                        S.use(v, ("of", lp))
                        i = v.tensor_tensor(of_sb[lp][:, :], ps_trC[lp][:, :], xt_sb[lp][:, :],
                                            AL.add)
                        EV[("fv", li)] = S.inc(i, "sv")
                        S.setfree(("ps_trC", lp), EV[("fv", li)])
                        S.setfree(("xt", lp), EV[("fv", li)])
                    vb(_v)

                    def _y2(sy, q=q, b=b, li=li, lp=lp, t0=t0):
                        S.wait(sy, EV[("fv", li)])
                        i = sy.dma_start(out_d.ap()[b, t0:t0 + 128, :], of_sb[lp][:, :])
                        EV[("ow", li)] = S.inc(i, "ss", 16)
                        S.setfree(("of", lp), EV[("ow", li)])
                    yb(_y2)
            pending_final[0] = _emit_final
            pending_xt[0] = _emit_xt

        pending_xt[0]()
        pending_final[0]()

        def _fin(sy):
            sy.wait_ge(S.h["ss"], S.c["ss"])
            sy.wait_ge(S.h["sg"], S.c["sg"])
            sy.wait_ge(S.h["sa"], S.c["sa"])
        yb(_fin)
        esC.close()
    return nc


_NC_CACHE = {}


def _enable_ldw_opt():
    try:
        from concourse.compiler_utils import get_compiler_flags, set_compiler_flags
        flags = get_compiler_flags()
        flags = [f.replace("--enable-ldw-opt=false", "--enable-ldw-opt=true")
                 for f in flags]
        set_compiler_flags(flags)
    except Exception:
        pass


def kernel(L0_attn_x, L0_attn_Wq, L0_attn_Wk, L0_attn_Wv, L0_attn_Wo, L0_attn_bo,
           L0_ffn_router_w, L0_ffn_wg_r, L0_ffn_wu_r, L0_ffn_wd_r,
           L0_ffn_wg_s, L0_ffn_wu_s, L0_ffn_wd_s):
    x = np.ascontiguousarray(np.asarray(L0_attn_x, dtype=np.float32))
    consts = {
        "wq": np.ascontiguousarray(np.asarray(L0_attn_Wq, np.float32)),
        "wk": np.ascontiguousarray(np.asarray(L0_attn_Wk, np.float32)),
        "wv": np.ascontiguousarray(np.asarray(L0_attn_Wv, np.float32)),
        "wo": np.ascontiguousarray(np.asarray(L0_attn_Wo, np.float32)),
        "bo": np.ascontiguousarray(np.asarray(L0_attn_bo, np.float32)),
        "rw": np.ascontiguousarray(np.asarray(L0_ffn_router_w, np.float32)),
        "wg_r": np.ascontiguousarray(np.asarray(L0_ffn_wg_r, np.float32)),
        "wu_r": np.ascontiguousarray(np.asarray(L0_ffn_wu_r, np.float32)),
        "wd_r": np.ascontiguousarray(np.asarray(L0_ffn_wd_r, np.float32)),
        "wg_s": np.ascontiguousarray(np.asarray(L0_ffn_wg_s, np.float32)),
        "wu_s": np.ascontiguousarray(np.asarray(L0_ffn_wu_s, np.float32)),
        "wd_s": np.ascontiguousarray(np.asarray(L0_ffn_wd_s, np.float32)),
        "cident": np.eye(128, dtype=ml_dtypes.bfloat16),
        "cmaskt": np.tile(np.triu(np.ones((128, 128))), (1, 4)).astype(ml_dtypes.bfloat16),
    }
    _enable_ldw_opt()
    if "nc" not in _NC_CACHE:
        _NC_CACHE["nc"] = build_program()
    nc = _NC_CACHE["nc"]
    in_maps = []
    for c in range(8):
        m = dict(consts)
        m["x"] = x[2 * c:2 * c + 2]
        in_maps.append(m)
    res = run_bass_kernel_spmd(nc, in_maps, core_ids=list(range(8)))
    return np.concatenate([r["out"] for r in res.results], axis=0)


if __name__ == "__main__":
    print("smoke build...")
    nc = build_program()
    print("built, instructions:", nc.n_instructions())


# version 16
# speedup vs baseline: 1.0754x; 1.0754x over previous
import numpy as np
import ml_dtypes
from contextlib import ExitStack

import concourse.bass as bass
import concourse.mybir as mybir
from concourse.bass_utils import run_bass_kernel_spmd
from concourse.alu_op_type import AluOpType as AL

F32 = mybir.dt.float32
BF16 = mybir.dt.bfloat16
AF = mybir.ActivationFunctionType
AX = mybir.AxisListType

B_LOC = 2
T = 2048
D = 1024
NBLK = 32
HP = 8
NR, MIDR = 16, 256
NS, MIDS = 2, 512
NQ, TQ = 4, 1024
EPS_LN = 1e-5
EPS_Z = 1e-6


class Ctx:

    def __init__(self, nc, es):
        self.nc = nc
        names = ["sg", "ss", "sa", "st", "sv", "sc"]
        self.h = {n: es.enter_context(nc.semaphore(n)) for n in names}
        self.c = {n: 0 for n in names}
        self.free = {}

    def inc(self, instr, n, amt=1):
        self.c[n] += amt
        instr.then_inc(self.h[n], amt)
        return (n, self.c[n])

    def now(self, n):
        return (n, self.c[n])

    def wait(self, eng, evt):
        if evt is None:
            return
        n, v = evt
        assert n in self.h and v <= self.c[n], f"bad wait {evt} (cnt={self.c.get(n)})"
        if v > 0:
            eng.wait_ge(self.h[n], v)

    def use(self, eng, res):
        self.wait(eng, self.free.get(res))

    def setfree(self, res, evt):
        self.free[res] = evt


class DB:

    def __init__(self, S, sem):
        self.S = S
        self.sem = sem
        self.items = []

    def add(self, fn, amt=1):
        self.S.c[self.sem] += amt
        evt = (self.sem, self.S.c[self.sem])
        self.items.append((fn, amt))
        return evt

    def flush(self, body_fn):
        items, S, sem = self.items, self.S, self.sem

        def run(eng):
            for fn, amt in items:
                instr = fn(eng)
                instr.then_inc(S.h[sem], amt)
        body_fn(run)
        self.items = []


def build_program(dump=None):
    nc = bass.Bass("TRN2", target_bir_lowering=False)

    x_d = nc.dram_tensor("x", [B_LOC, T, D], F32, kind="ExternalInput")
    wq_d = nc.dram_tensor("wq", [D, D], F32, kind="ExternalInput")
    wk_d = nc.dram_tensor("wk", [D, D], F32, kind="ExternalInput")
    wv_d = nc.dram_tensor("wv", [D, D], F32, kind="ExternalInput")
    wo_d = nc.dram_tensor("wo", [D, D], F32, kind="ExternalInput")
    bo_d = nc.dram_tensor("bo", [D], F32, kind="ExternalInput")
    rw_d = nc.dram_tensor("rw", [D, NR], F32, kind="ExternalInput")
    wgr_d = nc.dram_tensor("wg_r", [NR, D, MIDR], F32, kind="ExternalInput")
    wur_d = nc.dram_tensor("wu_r", [NR, D, MIDR], F32, kind="ExternalInput")
    wdr_d = nc.dram_tensor("wd_r", [NR, MIDR, D], F32, kind="ExternalInput")
    wgs_d = nc.dram_tensor("wg_s", [NS, D, MIDS], F32, kind="ExternalInput")
    wus_d = nc.dram_tensor("wu_s", [NS, D, MIDS], F32, kind="ExternalInput")
    wds_d = nc.dram_tensor("wd_s", [NS, MIDS, D], F32, kind="ExternalInput")
    cid_d = nc.dram_tensor("cident", [128, 128], BF16, kind="ExternalInput")
    cmk_d = nc.dram_tensor("cmaskt", [128, 512], BF16, kind="ExternalInput")
    out_d = nc.dram_tensor("out", [B_LOC, T, D], F32, kind="ExternalOutput")

    scr_xn = nc.dram_tensor("scr_xn", [8, 128, 4096], BF16, kind="Internal")
    scr_at = nc.dram_tensor("scr_at", [8, 128, 4096], BF16, kind="Internal")
    scr_cb = nc.dram_tensor("scr_cb", [NR, 4096], BF16, kind="Internal")
    scr_dn = nc.dram_tensor("scr_dn", [8, 128, 6144], BF16, kind="Internal")

    es = ExitStack()
    with es:
        S = Ctx(nc, es)
        block = es.enter_context(nc.Block())
        tb, vb, sb, gb, yb = block.tensor, block.vector, block.scalar, block.gpsimd, block.sync

        ident = es.enter_context(nc.sbuf_tensor("ident", [128, 128], BF16))
        maskt = es.enter_context(nc.sbuf_tensor("maskt", [128, 512], BF16))
        bo_sb = es.enter_context(nc.sbuf_tensor("bo_sb", [128, 8], F32))
        comb_fm = es.enter_context(nc.sbuf_tensor("comb_fm", [16, 4096], BF16))

        esAB = ExitStack()
        xn_fm = esAB.enter_context(nc.sbuf_tensor("xn_fm", [128, 8 * 4096], BF16))
        wq_sb = esAB.enter_context(nc.sbuf_tensor("wq_sb", [128, 8 * D], BF16))
        wk_sb = esAB.enter_context(nc.sbuf_tensor("wk_sb", [128, 8 * D], BF16))
        wv_sb = esAB.enter_context(nc.sbuf_tensor("wv_sb", [128, 8 * D], BF16))
        dstg = [esAB.enter_context(nc.sbuf_tensor(f"dstg{i}", [128, 4096], BF16))
                for i in range(2)]

        def xn_ap(k, col, ncol):
            return xn_fm.ap()[:, k * 4096 + col: k * 4096 + col + ncol]

        EV = {}

        esA = ExitStack()
        x_sb = [esA.enter_context(nc.sbuf_tensor(f"x{i}", [128, D], F32)) for i in range(2)]
        xn_tm = [esA.enter_context(nc.sbuf_tensor(f"xnt{i}", [128, D], BF16)) for i in range(2)]
        sqj = esA.enter_context(nc.sbuf_tensor("sqj", [128, D], BF16))
        sstat = esA.enter_context(nc.sbuf_tensor("sstat", [128, 2 * NBLK], F32))
        mstat = esA.enter_context(nc.sbuf_tensor("mstat", [128, NBLK], F32))
        tvar = esA.enter_context(nc.sbuf_tensor("tvar", [128, NBLK], F32))
        tm2 = esA.enter_context(nc.sbuf_tensor("tm2", [128, NBLK], F32))
        rstdb = esA.enter_context(nc.sbuf_tensor("rstdb", [128, NBLK], F32))
        nmrb = esA.enter_context(nc.sbuf_tensor("nmrb", [128, NBLK], F32))
        rw_sb = esA.enter_context(nc.sbuf_tensor("rw_sb", [128, 8 * NR], BF16))
        e4 = [esA.enter_context(nc.sbuf_tensor(f"e4{i}", [128, 4 * NR], F32))
              for i in range(2)]
        ebuf2 = esA.enter_context(nc.sbuf_tensor("ebuf2", [128, 4 * NR], F32))
        t4a = esA.enter_context(nc.sbuf_tensor("t4a", [128, 4], F32))
        t4b = esA.enter_context(nc.sbuf_tensor("t4b", [128, 4], F32))
        comb16 = esA.enter_context(nc.sbuf_tensor("comb16", [128, NBLK * NR], BF16))
        ps_tr = [esA.enter_context(nc.psum_tensor(f"pstr{i}", [128, 1024], BF16))
                 for i in range(2)]
        ps_lg = [esA.enter_context(nc.psum_tensor(f"pslg{i}", [128, 4 * NR], F32))
                 for i in range(2)]

        def _g_consts(g):
            S.inc(g.dma_start(ident[:, :], cid_d.ap()), "sg", 16)
            S.inc(g.dma_start(maskt[:, :], cmk_d.ap()), "sg", 16)
            S.inc(g.dma_start(bo_sb[:, :], bo_d.ap().rearrange("(m p) -> p m", p=128), allow_slow_non_contiguous=True),
                  "sg", 16)
            EV["consts"] = S.now("sg")
            S.inc(g.dma_start(rw_sb.ap().rearrange("p (k e) -> p k e", k=8),
                              rw_d.ap().rearrange("(k p) e -> p k e", p=128)), "sg", 16)
            EV["rwload"] = S.now("sg")
            for w_sb, w_d, key in ((wq_sb, wq_d, "wq"), (wk_sb, wk_d, "wk"),
                                   (wv_sb, wv_d, "wv")):
                S.inc(g.dma_start(w_sb.ap().rearrange("p (k m) -> p k m", k=8),
                                  w_d.ap().rearrange("(k p) m -> p k m", p=128)), "sg", 16)
                EV[key] = S.now("sg")
        gb(_g_consts)

        dbp1 = DB(S, "sc")
        for blk in range(NBLK):
            p = blk % 2

            def _sq(s, blk=blk, p=p):
                S.wait(s, EV[("a1s1", blk)])
                return s.activation(sqj[:, :], x_sb[p][:, :], AF.Square,
                                    accum_out=sstat[:, 2 * blk + 1:2 * blk + 2])
            EV[("a1sq", blk)] = dbp1.add(_sq)
            S.setfree(("x_sb", p), EV[("a1sq", blk)])
        for blk in range(NBLK):
            b, t0 = blk // 16, (blk % 16) * 128
            p = blk % 2

            def _y(sy, blk=blk, b=b, t0=t0, p=p):
                if blk >= 2:
                    S.wait(sy, EV[("a1sq", blk - 2)])
                i = sy.dma_start(x_sb[p][:, :], x_d.ap()[b, t0:t0 + 128, :])
                EV[("xld", blk)] = S.inc(i, "ss", 16)
            yb(_y)

            def _v1(v, blk=blk, p=p):
                S.wait(v, EV[("xld", blk)])
                v.memset(sstat[:, 2 * blk + 1:2 * blk + 2], 0.0)
                EV[("a1s1", blk)] = S.inc(
                    v.reduce_sum(sstat[:, 2 * blk:2 * blk + 1], x_sb[p][:, :],
                                 axis=AX.X), "sv")
            vb(_v1)
        dbp1.flush(sb)

        def _vstats(v):
            S.wait(v, EV[("a1sq", NBLK - 1)])
            s1v = bass.AP(sstat, 0, [[2 * NBLK, 128], [2, NBLK]])
            s2v = bass.AP(sstat, 1, [[2 * NBLK, 128], [2, NBLK]])
            v.tensor_scalar(mstat[:, :], s1v, 1.0 / D, None, AL.mult)
            v.tensor_scalar(tvar[:, :], s2v, 1.0 / D, None, AL.mult)
            v.drain()
            v.tensor_mul(tm2[:, :], mstat[:, :], mstat[:, :])
            v.drain()
            v.tensor_sub(tvar[:, :], tvar[:, :], tm2[:, :])
            v.drain()
            v.tensor_scalar(tvar[:, :], tvar[:, :], EPS_LN, None, AL.add)
            v.drain()
            EV["a1var"] = S.inc(v.reciprocal(tvar[:, :], tvar[:, :]), "sv")
        vb(_vstats)

        def _sstats(s):
            S.wait(s, EV["a1var"])
            EV["a1rs"] = S.inc(s.activation(rstdb[:, :], tvar[:, :], AF.Sqrt), "sc")
        sb(_sstats)

        def _vnmr(v):
            S.wait(v, EV["a1rs"])
            i = v.scalar_tensor_tensor(nmrb[:, :], mstat[:, :], -1.0, rstdb[:, :],
                                       AL.mult, AL.mult)
            EV["a1nm"] = S.inc(i, "sv")
        vb(_vnmr)

        dbp3 = DB(S, "sc")
        for blk in range(NBLK):
            p = blk % 2

            def _s3(s, blk=blk, p=p):
                S.wait(s, EV[("xld2", blk)])
                if blk == 0:
                    S.wait(s, EV["a1nm"])
                if blk >= 2:
                    S.wait(s, EV[("a1tr", blk - 2)])
                return s.activation(xn_tm[p][:, :], x_sb[p][:, :], AF.Identity,
                                    bias=nmrb[:, blk:blk + 1],
                                    scale=rstdb[:, blk:blk + 1])
            EV[("a1norm", blk)] = dbp3.add(_s3)
            S.setfree(("x_sb", p), EV[("a1norm", blk)])
        for blk in range(NBLK):
            b, t0 = blk // 16, (blk % 16) * 128
            p = blk % 2
            col = b * 2048 + t0

            def _y(sy, blk=blk, b=b, t0=t0, p=p):
                if blk >= 2:
                    S.wait(sy, EV[("a1norm", blk - 2)])
                else:
                    S.wait(sy, EV[("a1sq", NBLK - 2 + blk)])
                i = sy.dma_start(x_sb[p][:, :], x_d.ap()[b, t0:t0 + 128, :])
                EV[("xld2", blk)] = S.inc(i, "ss", 16)
            yb(_y)

            def _t1(t, blk=blk, p=p):
                if blk == 0:
                    S.wait(t, EV["consts"])
                S.wait(t, EV[("a1norm", blk)])
                S.use(t, ("ps_tr", p))
                for k in range(8):
                    i = t.transpose(ps_tr[p][:, k * 128:(k + 1) * 128],
                                    xn_tm[p][:, k * 128:(k + 1) * 128], ident[:, :])
                EV[("a1tr", blk)] = S.inc(i, "st")
                S.setfree(("xn_tm", p), EV[("a1tr", blk)])
            tb(_t1)

            def _v4(v, blk=blk, col=col, p=p):
                S.wait(v, EV[("a1tr", blk)])
                i = v.tensor_copy(
                    xn_fm.ap().rearrange("p (k c) -> p k c", k=8)[:, :, col:col + 128],
                    ps_tr[p].ap().rearrange("p (k c) -> p k c", k=8))
                EV[("a1trev", blk)] = S.inc(i, "sv")
                S.setfree(("ps_tr", p), EV[("a1trev", blk)])
            vb(_v4)

        dbp3.flush(sb)

        def _spill_xn(s):
            S.wait(s, EV[("a1trev", NBLK - 1)])
            for k in range(8):
                i = s.dma_start(scr_xn.ap()[k], xn_fm.ap()[:, k * 4096:(k + 1) * 4096])
                S.inc(i, "sa", 16)
            EV["xnspill"] = S.now("sa")
        sb(_spill_xn)

        if dump == "xn":
            def _dump(g):
                S.wait(g, EV[("a1trev", NBLK - 1)])
                dst = bass.AP(out_d, 0, [[32768, 128], [1, 32768]])
                S.inc(g.dma_start(dst, xn_fm.ap()[:, :]), "sg", 16)
                EV["dmp"] = S.now("sg")
            gb(_dump)

            def _fin(sy):
                S.wait(sy, EV["dmp"])
                S.wait(sy, EV["xnspill"])
            yb(_fin)
            esA.close()
            esAB.close()
            return nc

        dba2 = DB(S, "sc")
        for grp in range(8):
            pg = grp % 2

            def _s(s, grp=grp, pg=pg):
                S.wait(s, EV[("a2lg", grp)])
                if grp >= 2:
                    S.wait(s, EV[("a2c", grp - 2)])
                return s.activation(e4[pg][:, :], ps_lg[pg][:, :], AF.Exp)
            EV[("a2e", grp)] = dba2.add(_s)
            S.setfree(("ps_lg", pg), EV[("a2e", grp)])
        for grp in range(8):
            pg = grp % 2

            def _t(t, grp=grp, pg=pg):
                if grp == 0:
                    S.wait(t, EV[("a1trev", NBLK - 1)])
                    S.wait(t, EV["rwload"])
                if grp >= 2:
                    S.wait(t, EV[("a2e", grp - 2)])
                for j in range(4):
                    blk = grp * 4 + j
                    b, t0 = blk // 16, (blk % 16) * 128
                    col = b * 2048 + t0
                    for k in range(8):
                        i = t.matmul(ps_lg[pg][:, j * NR:(j + 1) * NR],
                                     xn_ap(k, col, 128),
                                     rw_sb[:, k * NR:(k + 1) * NR],
                                     start=(k == 0), stop=(k == 7))
                EV[("a2lg", grp)] = S.inc(i, "st")
            tb(_t)

            def _v2(v, grp=grp, pg=pg):
                S.wait(v, EV[("a2e", grp)])
                ev = e4[pg].ap().rearrange("p (j e) -> p j e", e=NR)
                e2v = ebuf2.ap().rearrange("p (j e) -> p j e", e=NR)
                v.tensor_reduce(t4a[:, :], ev, AX.X, AL.max)
                v.drain()
                m1b = t4a.ap()[:, :, None].broadcast_to([128, 4, NR])
                v.tensor_tensor(e2v, ev, m1b, AL.is_lt)
                v.tensor_mul(e2v, e2v, ev)
                v.drain()
                v.tensor_reduce(t4b[:, :], e2v, AX.X, AL.max)
                v.drain()
                m2b = t4b.ap()[:, :, None].broadcast_to([128, 4, NR])
                v.tensor_tensor(e2v, ev, m2b, AL.is_ge)
                v.tensor_add(t4a[:, :], t4a[:, :], t4b[:, :])
                v.drain()
                v.reciprocal(t4a[:, :], t4a[:, :])
                v.tensor_mul(e2v, e2v, ev)
                v.drain()
                rb = t4a.ap()[:, :, None].broadcast_to([128, 4, NR])
                i = v.tensor_tensor(
                    comb16.ap().rearrange("p (j e) -> p j e", e=NR)[:, grp * 4:grp * 4 + 4],
                    e2v, rb, AL.mult)
                EV[("a2c", grp)] = S.inc(i, "sv")
                S.setfree(("e4", pg), EV[("a2c", grp)])
            vb(_v2)

        dba2.flush(sb)

        for grp in range(4):
            def _t(t, grp=grp):
                S.wait(t, EV[("a2c", grp * 2 + 1)])
                S.use(t, ("ps_tr", grp % 2))
                for j in range(8):
                    blk = grp * 8 + j
                    i = t.transpose(ps_tr[grp % 2][0:16, j * 128:(j + 1) * 128],
                                    comb16[:, blk * NR:(blk + 1) * NR], ident[:, :])
                EV[("a2ctr", grp)] = S.inc(i, "st")
            tb(_t)

            def _v(v, grp=grp):
                S.wait(v, EV[("a2ctr", grp)])
                i = v.tensor_copy(comb_fm.ap()[:, grp * 1024:(grp + 1) * 1024],
                                  ps_tr[grp % 2].ap()[0:16, :])
                EV[("a2ctrev", grp)] = S.inc(i, "sv")
                S.setfree(("ps_tr", grp % 2), EV[("a2ctrev", grp)])
            vb(_v)


        def _spill_cb(s):
            S.wait(s, EV[("a2ctrev", 3)])
            i = s.dma_start(scr_cb.ap()[:, :], comb_fm.ap()[:, :])
            EV["cbspill"] = S.inc(i, "sa", 16)
        sb(_spill_cb)

        if dump == "comb":
            def _dump(g):
                S.wait(g, EV[("a2c", 7)])
                dst = bass.AP(out_d, 0, [[NBLK * NR, 128], [1, NBLK * NR]])
                S.inc(g.dma_start(dst, comb16.ap()[:, :]), "sg", 16)
                EV["dmp"] = S.now("sg")
            gb(_dump)

            def _fin(sy):
                S.wait(sy, EV["dmp"])
                S.wait(sy, EV["cbspill"])
            yb(_fin)
            esA.close()
            esAB.close()
            return nc

        items = [("wdr", e) for e in range(NR)] + [("wds", e2) for e2 in range(NS)] \
            + [("wo", 0), ("wo", 1)]
        for idx, (kind, e) in enumerate(items):
            p = idx % 2

            def _g(g, kind=kind, e=e, p=p, idx=idx):
                S.use(g, ("dstg", p))
                if kind == "wdr":
                    src = wdr_d.ap()[e].rearrange("(k p) c -> p k c", p=128)
                    dst = dstg[p].ap()[:, 0:2048].rearrange("p (k c) -> p k c", k=2)
                elif kind == "wds":
                    src = wds_d.ap()[e].rearrange("(k p) c -> p k c", p=128)
                    dst = dstg[p].ap().rearrange("p (k c) -> p k c", k=4)
                else:
                    src = wo_d.ap().rearrange("(k p) c -> p k c", p=128)[:, e * 4:(e + 1) * 4, :]
                    dst = dstg[p].ap().rearrange("p (k c) -> p k c", k=4)
                EV[("ppr", idx)] = S.inc(g.dma_start(dst, src), "sg", 16)
            gb(_g)

            def _s(s, kind=kind, e=e, p=p, idx=idx):
                S.wait(s, EV[("ppr", idx)])
                nk = 2 if kind == "wdr" else 4
                for dm in range(8):
                    off = {"wdr": e * 256, "wds": 4096 + e * 512, "wo": 5120 + e * 512}[kind]
                    src = bass.AP(dstg[p], dm * 128, [[4096, 128], [1024, nk], [1, 128]])
                    dst = bass.AP(scr_dn, dm * 128 * 6144 + off,
                                  [[6144, 128], [128, nk], [1, 128]])
                    i = s.dma_start(dst, src)
                    S.inc(i, "sa", 16)
                EV[("ppw", idx)] = S.now("sa")
                S.setfree(("dstg", p), EV[("ppw", idx)])
            sb(_s)
        EV["ppdone"] = EV[("ppw", len(items) - 1)]

        esA.close()
        esB = ExitStack()
        qraw = esB.enter_context(nc.sbuf_tensor("qraw", [128, T], BF16))
        kraw = esB.enter_context(nc.sbuf_tensor("kraw", [128, T], BF16))
        etmp = [esB.enter_context(nc.sbuf_tensor(f"etmp{i}", [128, T], BF16))
                for i in range(2)]
        mtmp = [esB.enter_context(nc.sbuf_tensor(f"mtmp{i}", [128, T], BF16))
                for i in range(2)]
        qf = esB.enter_context(nc.sbuf_tensor("qf", [128, T], BF16))
        kf = esB.enter_context(nc.sbuf_tensor("kf", [128, T], BF16))
        vf = esB.enter_context(nc.sbuf_tensor("vf", [128, T], BF16))
        q_tm = esB.enter_context(nc.sbuf_tensor("q_tm", [128, T], BF16))
        k_tm = esB.enter_context(nc.sbuf_tensor("k_tm", [128, T], BF16))
        v_tm = esB.enter_context(nc.sbuf_tensor("v_tm", [128, T], BF16))
        apack = esB.enter_context(nc.sbuf_tensor("apack", [128, T], BF16))
        aofm = esB.enter_context(nc.sbuf_tensor("aofm", [128, T], BF16))
        cumst = esB.enter_context(nc.sbuf_tensor("cumst", [128, 1024], F32))
        zt = esB.enter_context(nc.sbuf_tensor("zt", [128, 1024], F32))
        zq = esB.enter_context(nc.sbuf_tensor("zq", [128, 32], F32))
        sc_sb = [esB.enter_context(nc.sbuf_tensor(f"scsb{i}", [128, 1024], BF16))
                 for i in range(2)]
        st_b = [esB.enter_context(nc.sbuf_tensor(f"st_b{i}", [128, 128], BF16))
                for i in range(2)]
        ps_p = [esB.enter_context(nc.psum_tensor(f"psp{i}", [128, 512], F32))
                for i in range(2)]
        ps_sc = [esB.enter_context(nc.psum_tensor(f"pssc{i}", [128, 512], F32))
                 for i in range(2)]
        ps_un = esB.enter_context(nc.psum_tensor("psun", [128, 512], F32))
        ps_st = [esB.enter_context(nc.psum_tensor(f"psst{i}", [128, 64], F32))
                 for i in range(2)]
        ps_trB = esB.enter_context(nc.psum_tensor("pstrB", [128, 1024], BF16))

        FENCE_A = [EV[("a2ctrev", 3)], EV[("a2ctr", 3)], EV[("a2e", 7)]]

        fidx = 0
        ridx = 0

        for b in range(B_LOC):
            def _vz(v, b=b):
                S.use(v, "cumst")
                if b == 0:
                    for e_ in FENCE_A:
                        S.wait(v, e_)
                    S.inc(v.memset(st_b[0][:, :], 0.0), "sv")
                    S.inc(v.memset(st_b[1][:, :], 0.0), "sv")
                S.inc(v.memset(cumst[:, :], 0.0), "sv")
            vb(_vz)

            for hp in range(HP):
                hpi = b * HP + hp
                dbhp = DB(S, "sc")
                qbase = fidx
                ubase = 2 * qbase
                dbvev = DB(S, "sv")
                for roff, (raw_, mslot) in ((0, (qraw, 0)), (4, (kraw, 1))):
                    for hh in range(2):
                        frev = S.free.get(("etmp", mslot)) if hh == 0 else None

                        def _s(s, raw_=raw_, mslot=mslot, hh=hh, frev=frev,
                               roff=roff, ubase=ubase):
                            S.wait(s, EV[("pevac", ubase + roff + 2 * hh + 1)])
                            S.wait(s, frev)
                            cl = hh * 1024
                            return s.activation(etmp[mslot][:, cl:cl + 1024],
                                                raw_[:, cl:cl + 1024], AF.Exp)
                        EV[("fexp", (hpi, mslot, hh))] = dbhp.add(_s)
                for j6, (dst, dname) in enumerate(((qraw, "qraw"), (qraw, "qraw"),
                                                   (kraw, "kraw"), (kraw, "kraw"),
                                                   (vf, "vf"), (vf, "vf"))):
                    half = j6 % 2
                    for n in range(2):
                        u = ubase + 2 * j6 + n
                        slot = u % 2

                        frev = S.free.get(dname) if (half == 0 and n == 0) else None

                        if j6 < 4:
                            def _vv(v, dst=dst, half=half, n=n, slot=slot, u=u,
                                    frev=frev):
                                S.wait(v, EV[("pfill", u)])
                                if u == 0:
                                    for e_ in FENCE_A:
                                        S.wait(v, e_)
                                S.wait(v, frev)
                                return v.tensor_copy(
                                    dst[:, half * 1024 + n * 512:
                                        half * 1024 + (n + 1) * 512],
                                    ps_p[slot][:, :])
                            EV[("pevac", u)] = dbvev.add(_vv)
                            S.setfree(("ps_p", slot), EV[("pevac", u)])
                        else:
                            def _s(s, dst=dst, half=half, n=n, slot=slot, u=u,
                                   frev=frev):
                                S.wait(s, EV[("pfill", u)])
                                S.wait(s, frev)
                                return s.copy(
                                    dst[:, half * 1024 + n * 512:
                                        half * 1024 + (n + 1) * 512],
                                    ps_p[slot][:, :])
                            EV[("pevac", u)] = dbhp.add(_s)
                            S.setfree(("ps_p", slot), EV[("pevac", u)])

                for (wsb, wkey) in ((wq_sb, "wq"), (wk_sb, "wk"), (wv_sb, "wv")):
                    for half in range(2):
                        myf = fidx

                        def _t(t, wsb=wsb, wkey=wkey, half=half, myf=myf,
                               b=b, hp=hp):
                            u0 = 2 * myf
                            if myf == 0:
                                for e_ in FENCE_A:
                                    S.wait(t, e_)
                            if myf < 6:
                                S.wait(t, EV[wkey])
                                S.wait(t, EV[("a1trev", NBLK - 1)])
                            col = b * 2048 + half * 1024
                            for n in range(2):
                                u = u0 + n
                                if u >= 2:
                                    S.wait(t, EV[("pevac", u - 2)])
                                for k in range(8):
                                    i = t.matmul(ps_p[u % 2][:, :],
                                                 wsb[:, k * D + hp * 128: k * D + hp * 128 + 128],
                                                 xn_ap(k, col + n * 512, 512),
                                                 start=(k == 0), stop=(k == 7))
                                EV[("pfill", u)] = S.inc(i, "st")
                        tb(_t)
                        fidx += 1

                dbvev.flush(vb)

                for (raw, feat, fkey, mslot) in (
                        (qraw, qf, "qfeat", 0),
                        (kraw, kf, "kfeat", 1)):
                    for hh in range(2):
                        def _v2(v, raw=raw, feat=feat, fkey=fkey, hh=hh,
                                mslot=mslot, hpi=hpi):
                            S.wait(v, EV[("fexp", (hpi, mslot, hh))])
                            if hh == 0:
                                S.use(v, ("mtmp", mslot))
                                S.use(v, fkey)
                            cl = hh * 1024
                            v.tensor_scalar(mtmp[mslot][:, cl:cl + 1024],
                                            etmp[mslot][:, cl:cl + 1024], 1.0,
                                            None, AL.min)
                            i = v.scalar_tensor_tensor(feat[:, cl:cl + 1024],
                                                       raw[:, cl:cl + 1024], 0.0,
                                                       mtmp[mslot][:, cl:cl + 1024],
                                                       AL.max, AL.add)
                            EV[(fkey, (hpi, hh))] = S.inc(i, "sv")
                            if hh == 1:
                                S.setfree(("mtmp", mslot), EV[(fkey, (hpi, hh))])
                                S.setfree(("etmp", mslot), EV[(fkey, (hpi, hh))])
                        vb(_v2)
                dbhp.flush(sb)

                def _rawfree(hpi=hpi):
                    S.setfree("qraw", EV[("qfeat", (hpi, 1))])
                    S.setfree("kraw", EV[("kfeat", (hpi, 1))])
                _rawfree()

                for (src, dst, need, dkey) in ((qf, q_tm, "qfeat", "q_tm"),
                                               (kf, k_tm, "kfeat", "k_tm"),
                                               (vf, v_tm, "pevac", "v_tm")):
                    for r in range(2):
                        myr = ridx

                        def _t(t, src=src, r=r, myr=myr, need=need, dkey=dkey,
                               hpi=hpi):
                            if need == "pevac":
                                S.wait(t, EV[("pevac", ubase + 9 + 2 * r)])
                            else:
                                S.wait(t, EV[(need, (hpi, r))])
                            S.use(t, "ps_trB")
                            for j in range(8):
                                c = r * 8 + j
                                i = t.transpose(ps_trB[:, j * 128:(j + 1) * 128],
                                                src[:, c * 128:(c + 1) * 128], ident[:, :])
                            EV[("trB", myr)] = S.inc(i, "st")
                            if dkey == "v_tm" and r == 1:
                                S.setfree("vf", EV[("trB", myr)])
                        tb(_t)

                        def _v(v, dst=dst, r=r, myr=myr, dkey=dkey):
                            S.wait(v, EV[("trB", myr)])
                            if r == 0:
                                S.use(v, dkey)
                            i = v.tensor_copy(dst[:, r * 1024:(r + 1) * 1024], ps_trB[:, :])
                            EV[("trBev", myr)] = S.inc(i, "sv")
                            S.setfree("ps_trB", EV[("trBev", myr)])
                        vb(_v)
                        ridx += 1

                def _freefeat(hpi=hpi):
                    pass
                _freefeat()

                for h2 in range(2):
                    h_off = h2 * 64

                    def _vz(v, b=b, hp=hp, h2=h2, h_off=h_off, hpi=hpi):
                        S.wait(v, EV[("trBev", ridx - 1)])
                        km = bass.AP(k_tm, h_off, [[T, 128], [128, 16], [1, 64]])
                        cs = cumst.ap().rearrange("p (c d) -> p c d", d=64)
                        v.tensor_tensor(cs, cs, km, AL.add)
                        qm = bass.AP(q_tm, h_off, [[T, 128], [128, 16], [1, 64]])
                        ztv = zt.ap().rearrange("p (c d) -> p c d", d=64)
                        v.tensor_tensor(ztv, qm, cs, AL.mult)
                        v.tensor_reduce(zq[:, h2 * 16:h2 * 16 + 16], ztv, AX.X, AL.add)
                        v.drain()
                        v.tensor_scalar(zq[:, h2 * 16:h2 * 16 + 16],
                                        zq[:, h2 * 16:h2 * 16 + 16], EPS_Z, None, AL.add)
                        v.drain()
                        i = v.reciprocal(zq[:, h2 * 16:h2 * 16 + 16],
                                         zq[:, h2 * 16:h2 * 16 + 16])
                        EV[("zr", (hpi, h2))] = S.inc(i, "sv")
                        if hp == HP - 1 and h2 == 1:
                            S.setfree("cumst", EV[("zr", (hpi, h2))])
                    vb(_vz)


                def emit_scores(w, hpi=hpi):
                    for h2 in range(2):
                        h_off = h2 * 64

                        def _tsc(t, h2=h2, h_off=h_off, w=w, hpi=hpi):
                            if hpi == 0 and w == 0 and h2 == 0:
                                S.wait(t, EV["consts"])
                            S.use(t, ("ps_sc", h2))
                            for j in range(4):
                                c = 4 * w + j
                                i = t.matmul(ps_sc[h2][:, j * 128:(j + 1) * 128],
                                             kf[h_off:h_off + 64, c * 128:(c + 1) * 128],
                                             qf[h_off:h_off + 64, c * 128:(c + 1) * 128],
                                             start=True, stop=True)
                            EV[("sc", (hpi, w, h2))] = S.inc(i, "st")
                        tb(_tsc)

                        def _vsc(v, w=w, h2=h2, hpi=hpi):
                            S.wait(v, EV[("sc", (hpi, w, h2))])
                            S.use(v, ("sc_sb", (w % 2, h2)))
                            i = v.tensor_tensor(
                                sc_sb[w % 2][:, h2 * 512:(h2 + 1) * 512],
                                ps_sc[h2][:, :], maskt[:, :], AL.mult)
                            EV[("scev", (hpi, w, h2))] = S.inc(i, "sv")
                            S.setfree(("ps_sc", h2), EV[("scev", (hpi, w, h2))])
                        vb(_vsc)

                def emit_jloop(w, hpi=hpi):
                    for j in range(4):
                        c = 4 * w + j
                        for h2 in range(2):
                            h_off = h2 * 64

                            def _tc(t, h2=h2, h_off=h_off, w=w, j=j, c=c, hpi=hpi):
                                S.wait(t, EV[("scev", (hpi, w, h2))])
                                if j == 0:
                                    S.use(t, ("ps_un", h2))
                                reg = ps_un[:, j * 128 + h2 * 64: j * 128 + h2 * 64 + 64]
                                i = t.matmul(reg,
                                             sc_sb[w % 2][:, h2 * 512 + j * 128:
                                                          h2 * 512 + (j + 1) * 128],
                                             v_tm[:, c * 128 + h_off: c * 128 + h_off + 64],
                                             start=(h2 == 0),
                                             stop=(c == 0 and h2 == 1))
                                EV[("av", (hpi, h2, c))] = S.inc(i, "st")
                                if c < 15:
                                    if c == 0:
                                        S.use(t, ("ps_st", h2))
                                    else:
                                        S.wait(t, EV[("stc", (hpi, h2, c - 1))])
                                    tp = (0, 64) if h2 == 1 else (0, 0)
                                    i = t.matmul(ps_st[h2][h_off:h_off + 64, :],
                                                 k_tm[:, c * 128 + h_off: c * 128 + h_off + 64],
                                                 v_tm[:, c * 128 + h_off: c * 128 + h_off + 64],
                                                 start=(c == 0), stop=(c == 14),
                                                 tile_position=tp)
                                    EV[("dl", (hpi, h2, c))] = S.inc(i, "st")
                            tb(_tc)

                        if c > 0:
                            def _tt2(t, w=w, j=j, c=c, hpi=hpi):
                                S.wait(t, EV[("stc", (hpi, 0, c - 1))])
                                S.wait(t, EV[("stc", (hpi, 1, c - 1))])
                                i = t.matmul(ps_un[:, j * 128:(j + 1) * 128],
                                             qf[:, c * 128:(c + 1) * 128],
                                             st_b[(c - 1) % 2][:, :],
                                             start=False, stop=True)
                                EV[("t2", (hpi, c))] = S.inc(i, "st")
                                S.setfree(("stb", (c - 1) % 2),
                                          EV[("t2", (hpi, c))])
                            tb(_tt2)

                        if c < 15:
                            for h2 in range(2):
                                h_off = h2 * 64

                                def _sst(v, h_off=h_off, c=c, hpi=hpi, h2=h2):
                                    S.wait(v, EV[("dl", (hpi, h2, c))])
                                    S.use(v, ("stb", c % 2))
                                    i = v.tensor_copy(st_b[c % 2][h_off:h_off + 64,
                                                                  h_off:h_off + 64],
                                                      ps_st[h2][h_off:h_off + 64, :])
                                    EV[("stc", (hpi, h2, c))] = S.inc(i, "sv")
                                    if c == 14:
                                        S.setfree(("ps_st", h2),
                                                  EV[("stc", (hpi, h2, c))])
                                vb(_sst)

                    for h2 in range(2):
                        h_off = h2 * 64

                        def _vzev(v, h2=h2, h_off=h_off, w=w, hpi=hpi):
                            S.wait(v, EV[("t2", (hpi, 4 * w + 3))])
                            S.wait(v, EV[("zr", (hpi, h2))])
                            if w == 0 and h2 == 0:
                                S.use(v, "apack")
                            dst = bass.AP(apack, w * 512 + h_off,
                                          [[T, 128], [128, 4], [1, 64]])
                            src = bass.AP(ps_un, h2 * 64, [[512, 128], [128, 4], [1, 64]])
                            zb = zq.ap()[:, h2 * 16 + 4 * w: h2 * 16 + 4 * w + 4]
                            i = v.tensor_tensor(dst, src,
                                                zb[:, :, None].broadcast_to([128, 4, 64]),
                                                AL.mult)
                            EV[("zev", (hpi, w, h2))] = S.inc(i, "sv")
                            S.setfree(("ps_un", h2), EV[("zev", (hpi, w, h2))])
                            S.setfree(("sc_sb", (w % 2, h2)), EV[("zev", (hpi, w, h2))])
                        vb(_vzev)

                for w in range(4):
                    emit_scores(w)
                    emit_jloop(w)

                last_t2 = EV[("t2", (hpi, 15))]
                S.setfree("qfeat", last_t2)
                S.setfree("kfeat", last_t2)
                S.setfree("q_tm", last_t2)
                S.setfree("k_tm", last_t2)
                S.setfree("v_tm", last_t2)

                for r in range(2):
                    myr = ridx

                    def _t(t, r=r, myr=myr, hpi=hpi):
                        S.wait(t, EV[("zev", (hpi, 3, 1))])
                        S.use(t, "ps_trB")
                        for j in range(8):
                            c = r * 8 + j
                            i = t.transpose(ps_trB[:, j * 128:(j + 1) * 128],
                                            apack[:, c * 128:(c + 1) * 128], ident[:, :])
                        EV[("trB", myr)] = S.inc(i, "st")
                        if r == 1:
                            S.setfree("apack", EV[("trB", myr)])
                    tb(_t)

                    def _v(v, r=r, myr=myr):
                        S.wait(v, EV[("trB", myr)])
                        if r == 0:
                            S.use(v, "aofm")
                        i = v.tensor_copy(aofm[:, r * 1024:(r + 1) * 1024], ps_trB[:, :])
                        EV[("trBev", myr)] = S.inc(i, "sv")
                        S.setfree("ps_trB", EV[("trBev", myr)])
                    vb(_v)
                    ridx += 1

                def _sp(s, b=b, hp=hp, hpi=hpi):
                    S.wait(s, EV[("trBev", ridx - 1)])
                    i = s.dma_start(scr_at.ap()[hp][:, b * 2048:(b + 1) * 2048], aofm[:, :])
                    EV[("aspill", hpi)] = S.inc(i, "sa", 16)
                    S.setfree("aofm", EV[("aspill", hpi)])
                sb(_sp)

                if dump == "attn":
                    def _dmp(g, b=b, hp=hp, hpi=hpi):
                        S.wait(g, EV[("trBev", ridx - 1)])
                        S.wait(g, EV[("aspill", hpi)])
                        off = (hp * 2 + b) * 128 * 2048
                        dst = bass.AP(out_d, off, [[2048, 128], [1, 2048]])
                        i = g.dma_start(dst, aofm[:, :])
                        EV[("adump", hpi)] = S.inc(i, "sg", 16)
                        S.setfree("aofm", EV[("adump", hpi)])
                    gb(_dmp)

        EV["bdone_v"] = S.now("sv")
        EV["bdone_t"] = S.now("st")
        EV["bdone_s"] = S.now("sc")
        EV["bdone_a"] = S.now("sa")
        EV["bdone_g"] = S.now("sg")
        EV["bdone_y"] = S.now("ss")

        if dump == "attn":
            def _fin(sy):
                S.wait(sy, EV["bdone_a"])
                S.wait(sy, EV["bdone_g"])
                S.wait(sy, EV["bdone_y"])
            yb(_fin)
            esB.close()
            esAB.close()
            return nc

        esB.close()
        esAB.close()
        esC = ExitStack()
        h_sb = [esC.enter_context(nc.sbuf_tensor(f"h{e}", [128, 2048], BF16))
                for e in range(NR)]
        hs_sb = esC.enter_context(nc.sbuf_tensor("hs", [128, 8 * 1024], BF16))
        xn_q = esC.enter_context(nc.sbuf_tensor("xnq", [128, 8 * 1024], BF16))
        at_q = esC.enter_context(nc.sbuf_tensor("atq", [128, 8 * 1024], BF16))
        wgu = [esC.enter_context(nc.sbuf_tensor(f"wgu{i}", [128, 4096], BF16))
               for i in range(2)]
        dstr = [esC.enter_context(nc.sbuf_tensor(f"dstr{i}", [128, 6144], BF16))
                for i in range(2)]
        crep = [esC.enter_context(nc.sbuf_tensor(f"crep{i}", [128, 1024], BF16))
                for i in range(2)]
        sgb = [esC.enter_context(nc.sbuf_tensor(f"sgb{i}", [128, 1024], BF16))
               for i in range(2)]
        htmp = esC.enter_context(nc.sbuf_tensor("htmp", [128, 1024], BF16))
        outbf = esC.enter_context(nc.sbuf_tensor("outbf", [128, 8 * 1024], BF16))
        xt_sb = [esC.enter_context(nc.sbuf_tensor(f"xt{i}", [128, D], F32))
                 for i in range(2)]
        of_sb = [esC.enter_context(nc.sbuf_tensor(f"of{i}", [128, D], F32))
                 for i in range(2)]
        ps_up = [esC.enter_context(nc.psum_tensor(f"psup{i}", [128, 1024], F32))
                 for i in range(2)]
        ps_dn = esC.enter_context(nc.psum_tensor("psdn", [128, 1024], F32))
        ps_trC = [esC.enter_context(nc.psum_tensor(f"pstrC{i}", [128, 1024], BF16))
                  for i in range(2)]

        FENCE_B = [EV["bdone_v"], EV["bdone_t"], EV["bdone_s"], EV["bdone_a"],
                   EV["bdone_g"], EV["bdone_y"]]

        def fence(eng):
            for e_ in FENCE_B:
                S.wait(eng, e_)

        eidx = 0

        def up_fill(t, slot, wslot, coff, q, extra_waits):
            for e_ in extra_waits:
                S.wait(t, e_)
            S.use(t, ("ps_up", slot))
            for k in range(8):
                for n in range(2):
                    i = t.matmul(ps_up[slot][:, n * 512:(n + 1) * 512],
                                 wgu[wslot][:, k * 512 + coff: k * 512 + coff + 128],
                                 xn_q[:, k * 1024 + n * 512: k * 1024 + (n + 1) * 512],
                                 start=(k == 0), stop=(k == 7))
            return S.inc(i, "st")

        pending_final = [None]
        pending_xt = [None]
        for q in range(NQ):
            b = q // 2
            tq0 = (q % 2) * 1024

            if pending_xt[0] is not None:
                pending_xt[0]()
                pending_xt[0] = None

            def _yq(sy, q=q):
                if q == 0:
                    fence(sy)
                S.wait(sy, EV["xnspill"])
                S.use(sy, "xn_q")
                src = bass.AP(scr_xn, q * 1024, [[4096, 128], [128 * 4096, 8], [1, 1024]])
                i = sy.dma_start(xn_q.ap().rearrange("p (k c) -> p k c", k=8), src)
                EV[("xnq", q)] = S.inc(i, "ss", 16)
                S.use(sy, "at_q")
                src = bass.AP(scr_at, q * 1024, [[4096, 128], [128 * 4096, 8], [1, 1024]])
                i = sy.dma_start(at_q.ap().rearrange("p (k c) -> p k c", k=8), src)
                EV[("atq", q)] = S.inc(i, "ss", 16)
            yb(_yq)

            for e2 in range(NS):
                for proj, wd in ((0, wgs_d), (1, wus_d)):
                    def _g(g, wd=wd, e2=e2, proj=proj, q=q):
                        if q == 0 and e2 == 0 and proj == 0:
                            fence(g)
                        S.use(g, ("wgu", proj))
                        src = wd.ap()[e2].rearrange("(k p) m -> p k m", p=128)
                        i = g.dma_start(wgu[proj].ap().rearrange("p (k m) -> p k m", k=8),
                                        src)
                        EV[("shw", (q, e2, proj))] = S.inc(i, "sg", 16)
                    gb(_g)

                dbsh = DB(S, "sc")
                for m in range(4):
                    frev = S.free.get(("sgb", 0)) if m == 0 else None

                    def _s(s, e2=e2, m=m, q=q, frev=frev):
                        S.wait(s, EV[("upf", (q, "sh", e2, m, 0))])
                        if q == 0 and e2 == 0 and m == 0:
                            fence(s)
                        if m == 0:
                            S.wait(s, frev)
                        else:
                            S.wait(s, EV[("hsev", (q, e2, m - 1))])
                        return s.activation(sgb[0][:, :], ps_up[0][:, :], AF.Silu)
                    EV[("shsilu", (q, e2, m))] = dbsh.add(_s)
                for m in range(4):
                    for proj in range(2):
                        def _t(t, e2=e2, m=m, proj=proj, q=q):
                            if q == 0 and e2 == 0 and m == 0 and proj == 0:
                                fence(t)
                            ew = [EV[("shw", (q, e2, proj))], EV[("xnq", q)]]
                            EV[("upf", (q, "sh", e2, m, proj))] = up_fill(
                                t, proj, proj, m * 128, q, ew)
                            if m == 3:
                                S.setfree(("wgu", proj),
                                          EV[("upf", (q, "sh", e2, m, proj))])
                        tb(_t)
                        if proj == 0:
                            S.setfree(("ps_up", 0), EV[("shsilu", (q, e2, m))])
                        else:
                            pass

                    def _v(v, e2=e2, m=m, q=q):
                        S.wait(v, EV[("upf", (q, "sh", e2, m, 1))])
                        S.wait(v, EV[("shsilu", (q, e2, m))])
                        if q == 0 and e2 == 0 and m == 0:
                            fence(v)
                        if e2 == 0 and m == 0:
                            S.use(v, "hs_sb")
                        col = (e2 * 4 + m) * 1024
                        i = v.tensor_tensor(hs_sb[:, col:col + 1024], ps_up[1][:, :],
                                            sgb[0][:, :], AL.mult)
                        EV[("hsev", (q, e2, m))] = S.inc(i, "sv")
                        S.setfree(("ps_up", 1), EV[("hsev", (q, e2, m))])
                    vb(_v)
                dbsh.flush(sb)

            if pending_final[0] is not None:
                pending_final[0]()
                pending_final[0] = None

            for e in range(NR):
                mye = eidx
                ep = mye % 2

                def _g(g, e=e, mye=mye, ep=ep, q=q):
                    S.use(g, ("wgu", ep))
                    src = wgr_d.ap()[e].rearrange("(k p) m -> p k m", p=128)
                    dst = bass.AP(wgu[ep], 0, [[4096, 128], [512, 8], [1, 256]])
                    S.inc(g.dma_start(dst, src), "sg", 16)
                    src = wur_d.ap()[e].rearrange("(k p) m -> p k m", p=128)
                    dst = bass.AP(wgu[ep], 256, [[4096, 128], [512, 8], [1, 256]])
                    EV[("rw", mye)] = S.inc(g.dma_start(dst, src), "sg", 16)
                    S.wait(g, EV["cbspill"])
                    S.use(g, ("crep", ep))
                    src = bass.AP(scr_cb, e * 4096 + q * 1024, [[0, 128], [1, 1024]])
                    EV[("crep", mye)] = S.inc(g.dma_start(crep[ep][:, :], src), "sg", 16)
                gb(_g)

                dbrt = DB(S, "sc")
                for m2 in range(2):
                    frev = S.free.get(("sgb", m2))

                    def _s(s, mye=mye, m=m2, slot=m2, frev=frev):
                        S.wait(s, EV[("rupf", (mye, m))])
                        S.wait(s, frev)
                        return s.activation(sgb[m][:, :], ps_up[slot][:, :], AF.Silu)
                    EV[("silu", (mye, m2))] = dbrt.add(_s)
                for rnd in range(4):
                    proj = 0 if rnd < 2 else 1
                    m = rnd % 2
                    slot = rnd % 2
                    coff = proj * 256 + m * 128

                    def _t(t, e=e, mye=mye, ep=ep, rnd=rnd, m=m, slot=slot, coff=coff, q=q):
                        ew = [EV[("rw", mye)], EV[("xnq", q)]]
                        evt = up_fill(t, slot, ep, coff, q, ew)
                        EV[("rupf", (mye, rnd))] = evt
                        if rnd == 3:
                            S.setfree(("wgu", ep), evt)
                            if e == NR - 1:
                                EV[("upte", q)] = evt
                                S.setfree("xn_q", evt)
                    tb(_t)
                    if rnd < 2:
                        S.setfree(("ps_up", rnd), EV[("silu", (mye, rnd))])

                    if rnd < 2:
                        pass
                    else:
                        def _v(v, e=e, mye=mye, ep=ep, m=m, rnd=rnd, slot=slot, q=q):
                            S.wait(v, EV[("rupf", (mye, rnd))])
                            S.wait(v, EV[("crep", mye)])
                            if rnd == 2:
                                S.use(v, ("h_sb", e))
                            i1 = v.tensor_tensor(htmp[:, :], ps_up[slot][:, :],
                                                 crep[ep][:, :], AL.mult)
                            S.setfree(("ps_up", slot), S.inc(i1, "sv"))
                            S.wait(v, EV[("silu", (mye, m))])
                            i = v.tensor_tensor(h_sb[e][:, m * 1024:(m + 1) * 1024],
                                                htmp[:, :], sgb[m][:, :], AL.mult)
                            EV[("hmul", (mye, m))] = S.inc(i, "sv")
                            S.setfree(("sgb", m), EV[("hmul", (mye, m))])
                            if m == 1:
                                S.setfree(("crep", ep), EV[("hmul", (mye, m))])
                                if e == NR - 1:
                                    EV[("hall", q)] = EV[("hmul", (mye, m))]
                        vb(_v)
                dbrt.flush(sb)
                eidx += 1

            dbdn = DB(S, "sc")
            for dm in range(8):
                mydm = q * 8 + dm

                frev = S.free.get("outbf") if dm == 0 else None

                def _sdn(s, dm=dm, mydm=mydm, q=q, frev=frev):
                    S.wait(s, EV[("dnmm", mydm)])
                    S.wait(s, frev)
                    return s.activation(outbf[:, dm * 1024:(dm + 1) * 1024],
                                        ps_dn[:, :], AF.Identity,
                                        bias=bo_sb[:, dm:dm + 1])
                EV[("dnev", mydm)] = dbdn.add(_sdn)
            for dm in range(8):
                mydm = q * 8 + dm
                dp = mydm % 2

                def _y(sy, dm=dm, mydm=mydm, dp=dp):
                    S.wait(sy, EV["ppdone"])
                    S.use(sy, ("dstr", dp))
                    i = sy.dma_start(dstr[dp][:, :], scr_dn.ap()[dm])
                    EV[("dst", mydm)] = S.inc(i, "ss", 16)
                yb(_y)

                def _t(t, dm=dm, mydm=mydm, dp=dp, q=q):
                    S.wait(t, EV[("dst", mydm)])
                    S.wait(t, EV[("hall", q)])
                    S.wait(t, EV[("hsev", (q, NS - 1, 3))])
                    S.wait(t, EV[("atq", q)])
                    S.use(t, "ps_dn")
                    srcs = []
                    for e in range(NR):
                        for k2 in range(2):
                            srcs.append(((e * 2 + k2) * 128,
                                         h_sb[e].ap()[:, k2 * 1024:(k2 + 1) * 1024]))
                    for e2 in range(NS):
                        for k2 in range(4):
                            srcs.append((4096 + (e2 * 4 + k2) * 128,
                                         hs_sb.ap()[:, (e2 * 4 + k2) * 1024:
                                                    (e2 * 4 + k2 + 1) * 1024]))
                    for k in range(8):
                        srcs.append((5120 + k * 128, at_q.ap()[:, k * 1024:(k + 1) * 1024]))
                    for si, (off, rhs) in enumerate(srcs):
                        for n in range(2):
                            i = t.matmul(ps_dn[:, n * 512:(n + 1) * 512],
                                         dstr[dp][:, off:off + 128],
                                         rhs[:, n * 512:(n + 1) * 512],
                                         start=(si == 0), stop=(si == len(srcs) - 1))
                    EV[("dnmm", mydm)] = S.inc(i, "st")
                    S.setfree("ps_dn", EV[("dnev", mydm)])
                    S.setfree(("dstr", dp), EV[("dnmm", mydm)])
                    if dm == 7:
                        EV[("dnte", q)] = EV[("dnmm", mydm)]
                        for e in range(NR):
                            S.setfree(("h_sb", e), EV[("dnte", q)])
                        S.setfree("hs_sb", EV[("dnte", q)])
                        S.setfree("at_q", EV[("dnte", q)])
                tb(_t)


            dbdn.flush(sb)

            def _emit_xt(q=q, b=b, tq0=tq0, ls=(0, 1)):
                for l in ls:
                    li = q * 8 + l
                    lp = li % 2
                    t0 = tq0 + l * 128

                    def _y(sy, q=q, b=b, l=l, li=li, lp=lp, t0=t0):
                        S.use(sy, ("xt", lp))
                        i = sy.dma_start(xt_sb[lp][:, :], x_d.ap()[b, t0:t0 + 128, :])
                        EV[("xt", li)] = S.inc(i, "ss", 16)
                    yb(_y)

            def _emit_final(q=q, b=b, tq0=tq0):
                for l in range(8):
                    li = q * 8 + l
                    lp = li % 2
                    t0 = tq0 + l * 128
                    if l >= 2:
                        _emit_xt(q, b, tq0, ls=(l,))

                    def _t(t, q=q, l=l, li=li, lp=lp):
                        S.wait(t, EV[("dnev", q * 8 + 7)])
                        S.use(t, ("ps_trC", lp))
                        for dm in range(8):
                            i = t.transpose(ps_trC[lp][:, dm * 128:(dm + 1) * 128],
                                            outbf[:, dm * 1024 + l * 128: dm * 1024 + l * 128 + 128],
                                            ident[:, :])
                        EV[("ftr", li)] = S.inc(i, "st")
                        if l == 7:
                            S.setfree("outbf", EV[("ftr", li)])
                    tb(_t)

                    def _v(v, li=li, lp=lp):
                        S.wait(v, EV[("ftr", li)])
                        S.wait(v, EV[("xt", li)])
                        S.use(v, ("of", lp))
                        i = v.tensor_tensor(of_sb[lp][:, :], ps_trC[lp][:, :], xt_sb[lp][:, :],
                                            AL.add)
                        EV[("fv", li)] = S.inc(i, "sv")
                        S.setfree(("ps_trC", lp), EV[("fv", li)])
                        S.setfree(("xt", lp), EV[("fv", li)])
                    vb(_v)

                    def _y2(sy, q=q, b=b, li=li, lp=lp, t0=t0):
                        S.wait(sy, EV[("fv", li)])
                        i = sy.dma_start(out_d.ap()[b, t0:t0 + 128, :], of_sb[lp][:, :])
                        EV[("ow", li)] = S.inc(i, "ss", 16)
                        S.setfree(("of", lp), EV[("ow", li)])
                    yb(_y2)
            pending_final[0] = _emit_final
            pending_xt[0] = _emit_xt

        pending_xt[0]()
        pending_final[0]()

        def _fin(sy):
            sy.wait_ge(S.h["ss"], S.c["ss"])
            sy.wait_ge(S.h["sg"], S.c["sg"])
            sy.wait_ge(S.h["sa"], S.c["sa"])
        yb(_fin)
        esC.close()
    return nc


_NC_CACHE = {}


def _enable_ldw_opt():
    try:
        from concourse.compiler_utils import get_compiler_flags, set_compiler_flags
        flags = get_compiler_flags()
        flags = [f.replace("--enable-ldw-opt=false", "--enable-ldw-opt=true")
                 for f in flags]
        set_compiler_flags(flags)
    except Exception:
        pass


def kernel(L0_attn_x, L0_attn_Wq, L0_attn_Wk, L0_attn_Wv, L0_attn_Wo, L0_attn_bo,
           L0_ffn_router_w, L0_ffn_wg_r, L0_ffn_wu_r, L0_ffn_wd_r,
           L0_ffn_wg_s, L0_ffn_wu_s, L0_ffn_wd_s):
    x = np.ascontiguousarray(np.asarray(L0_attn_x, dtype=np.float32))
    consts = {
        "wq": np.ascontiguousarray(np.asarray(L0_attn_Wq, np.float32)),
        "wk": np.ascontiguousarray(np.asarray(L0_attn_Wk, np.float32)),
        "wv": np.ascontiguousarray(np.asarray(L0_attn_Wv, np.float32)),
        "wo": np.ascontiguousarray(np.asarray(L0_attn_Wo, np.float32)),
        "bo": np.ascontiguousarray(np.asarray(L0_attn_bo, np.float32)),
        "rw": np.ascontiguousarray(np.asarray(L0_ffn_router_w, np.float32)),
        "wg_r": np.ascontiguousarray(np.asarray(L0_ffn_wg_r, np.float32)),
        "wu_r": np.ascontiguousarray(np.asarray(L0_ffn_wu_r, np.float32)),
        "wd_r": np.ascontiguousarray(np.asarray(L0_ffn_wd_r, np.float32)),
        "wg_s": np.ascontiguousarray(np.asarray(L0_ffn_wg_s, np.float32)),
        "wu_s": np.ascontiguousarray(np.asarray(L0_ffn_wu_s, np.float32)),
        "wd_s": np.ascontiguousarray(np.asarray(L0_ffn_wd_s, np.float32)),
        "cident": np.eye(128, dtype=ml_dtypes.bfloat16),
        "cmaskt": np.tile(np.triu(np.ones((128, 128))), (1, 4)).astype(ml_dtypes.bfloat16),
    }
    _enable_ldw_opt()
    if "nc" not in _NC_CACHE:
        _NC_CACHE["nc"] = build_program()
    nc = _NC_CACHE["nc"]
    in_maps = []
    for c in range(8):
        m = dict(consts)
        m["x"] = x[2 * c:2 * c + 2]
        in_maps.append(m)
    res = run_bass_kernel_spmd(nc, in_maps, core_ids=list(range(8)))
    return np.concatenate([r["out"] for r in res.results], axis=0)


if __name__ == "__main__":
    print("smoke build...")
    nc = build_program()
    print("built, instructions:", nc.n_instructions())
